# version 09
# baseline (speedup 1.0000x reference)
import numpy as np
import ml_dtypes
from contextlib import ExitStack

import os
import concourse.bass as bass
import concourse.mybir as mybir
from concourse.bass_utils import run_bass_kernel_spmd

KPHASES = os.environ.get("KPHASES", "FULL")
KDBG = bool(int(os.environ.get("KDBG", "0")))

F32 = mybir.dt.float32
BF16 = mybir.dt.bfloat16
AF = mybir.ActivationFunctionType
ALU = mybir.AluOpType
AX = mybir.AxisListType

NSEQ = 2
T = 2048
D = 1024
NCH = 8
NSL = 4
SLT = 512
NTT = 4
EPS = 1e-5
SHIFT = 3.0

ENGINES = ("T", "V", "S", "G", "Y")


def build_nc():
    nc = bass.Bass("TRN2", target_bir_lowering=False, detect_race_conditions=False)

    dx = nc.dram_tensor("x", [NSEQ, T, D], F32, kind="ExternalInput").ap()
    dw = {n: nc.dram_tensor(n, [D, D], F32, kind="ExternalInput")
          for n in ("wq", "wk", "wv", "wl", "wo", "bwT")}
    dastk = nc.dram_tensor("astk", [D, 134], F32, kind="ExternalInput")
    dbstk = nc.dram_tensor("bstk", [128, D], F32, kind="ExternalInput")
    dbcols = nc.dram_tensor("bcols", [128, 48], F32, kind="ExternalInput").ap()
    drbcol = nc.dram_tensor("rbcol", [128, 1], F32, kind="ExternalInput").ap()
    dblrow = nc.dram_tensor("blrow", [1, D], F32, kind="ExternalInput").ap()
    dones = nc.dram_tensor("onesr", [1, 128], F32, kind="ExternalInput").ap()
    didb = nc.dram_tensor("identbf", [128, 128], BF16, kind="ExternalInput").ap()
    didf = nc.dram_tensor("identf", [128, 128], F32, kind="ExternalInput").ap()
    de16 = nc.dram_tensor("e16aug", [6, 128], F32, kind="ExternalInput").ap()
    dcst = nc.dram_tensor("consts", [128, 4], F32, kind="ExternalInput").ap()
    scratch_kind = "ExternalOutput" if KDBG else "Internal"
    dzs = nc.dram_tensor("z_scr", [NSEQ, NCH, 128, T], F32, kind=scratch_kind).ap()
    dx2 = nc.dram_tensor("x2_scr", [NSEQ * T, D], F32, kind=scratch_kind).ap()
    dout = nc.dram_tensor("out", [NSEQ, T, D], F32, kind="ExternalOutput").ap()
    if KDBG:
        ddbg_q = nc.dram_tensor("dbg_q", [128, NCH, T], BF16, kind="ExternalOutput").ap()
        ddbg_k = nc.dram_tensor("dbg_k", [128, NCH, T], BF16, kind="ExternalOutput").ap()
        ddbg_v = nc.dram_tensor("dbg_v", [128, 16, 8, 130], BF16, kind="ExternalOutput").ap()
        ddbg_a = nc.dram_tensor("dbg_a", [128, NCH, SLT], BF16, kind="ExternalOutput").ap()
        ddbg_w = nc.dram_tensor("dbg_w", [128, NCH, D], BF16, kind="ExternalOutput").ap()
        ddbg_x = nc.dram_tensor("dbg_x", [128, NCH, SLT], BF16, kind="ExternalOutput").ap()
        ddbg_t = nc.dram_tensor("dbg_t", [128, 2, D], BF16, kind="ExternalOutput").ap()
        ddbg_s = nc.dram_tensor("dbg_s", [128, 5, NTT], F32, kind="ExternalOutput").ap()
        ddbg_z0 = nc.dram_tensor("dbg_z0", [NCH, 128, T], F32, kind="ExternalOutput").ap()
        ddbg_l = nc.dram_tensor("dbg_l", [128, NTT, 6], F32, kind="ExternalOutput").ap()
        ddbg_g = nc.dram_tensor("dbg_g", [128, NTT, 6], F32, kind="ExternalOutput").ap()
        ddbg_e = nc.dram_tensor("dbg_e", [128, NTT, 6], F32, kind="ExternalOutput").ap()

    with ExitStack() as ctx:
        def sb(name, shape, dtype, **kw):
            return ctx.enter_context(nc.sbuf_tensor(name, shape, dtype, **kw))

        arena1 = sb("arena1", [128, 3, NCH, D], BF16)
        wl_sb = sb("wl_sb", [128, NCH, D], BF16)
        wq_sb, wk_sb, wv_sb = arena1[:, 0], arena1[:, 1], arena1[:, 2]
        wo_sb = arena1[:, 0]
        bw_sb = wl_sb[:]
        astk_a = arena1[:, 1]
        bstk_a = arena1[:, 2]

        bcols = sb("bcols_sb", [128, 48], F32)
        rbcol = sb("rbcol_sb", [128, 1], F32)
        ones_r = sb("ones_sb", [1, 128], F32)
        identb = sb("identb_sb", [128, 128], BF16)
        ones_bf = sb("ones_bf", [1, 128], BF16)
        blrow_bf = sb("blrow_bf", [1, D], BF16)
        identf = sb("identf_sb", [128, 128], F32)
        e16 = sb("e16_sb", [6, 128], F32)
        cst = sb("cst_sb", [128, 4], F32)

        xst = sb("xst", [128, NTT, D], F32)
        st1 = sb("st1", [128, NTT], F32)
        st2 = sb("st2", [128, NTT], F32)
        stm = sb("stm", [128, NTT], F32)
        stv = sb("stv", [128, NTT], F32)
        sti = sb("sti", [128, NTT], F32)
        stq = sb("stq", [128, NTT], F32)
        stn = sb("stn", [128, NTT], F32)
        xn_tok = sb("xn_tok", [128, 3, D], BF16)
        xn_fm = sb("xn_fm", [128, NCH, SLT], BF16)
        u_s = sb("u_s", [128, NCH, SLT], BF16)
        q_sb = sb("q_sb", [128, NCH, T], BF16)
        k_sb = sb("k_sb", [128, NCH, T], BF16)
        vt_sb = sb("vt_sb", [128, 16, 8, 130], BF16)
        pt_sb = sb("pt_sb", [128, 2, 2, SLT], BF16)
        pth = pt_sb.bitcast(F32).reshape([128, 1024])
        dummy = pth[:, 0:1024]
        zsp = sb("zsp", [128, NTT, 2], F32)
        lg_tok = sb("lg_tok", [128, NTT, 6], F32)
        ge_sb = sb("ge_sb", [128, NTT, 6], F32)
        g_t1 = sb("g_t1", [128, NTT, 6], F32)
        g_t2 = sb("g_t2", [128, NTT, 6], F32)
        g_m1 = sb("g_m1", [128, NTT], F32)
        g_m2 = sb("g_m2", [128, NTT], F32)
        g_s = sb("g_s", [128, NTT], F32)
        g_rd = sb("g_rd", [128, NTT], F32)
        w_tok = sb("w_tok", [128, NTT, 6], F32)

        z_sb = xst[:].rearrange("p a b -> p (a b)").rearrange(
            "p (m c) -> p m c", m=NCH)
        z_ld = z_sb
        kh = k_sb.bitcast(F32).reshape([128, 8192])
        z_tok = kh[:, 0:4096].rearrange("p (t d) -> p t d", t=NTT)
        y_tok = z_tok
        xt = kh[:, 4096:8192].rearrange("p (t d) -> p t d", t=NTT)
        x2r = xt
        qh = q_sb.bitcast(F32).reshape([128, 8192])
        x2_sl = qh[:, 0:4096].rearrange("p (t d) -> p t d", t=NTT)
        out_sb = qh[:, 4096:8192].rearrange("p (t d) -> p t d", t=NTT)
        vh = vt_sb.bitcast(F32).reshape([128, 8320])
        y_sb = vh[:, 0:4096].rearrange("p (m c) -> p m c", m=NCH)
        uh = u_s.bitcast(F32).reshape([128, 2048])
        xn2_t = uh[:, 0:1024]
        xn2_t4 = xst[:]
        fmh = xn_fm.bitcast(F32).reshape([128, 2048])
        den_sb = fmh[0:1, 0:1024]
        rden = fmh[0:1, 1024:2048]
        xth = xn_tok.bitcast(F32).reshape([128, 1536])
        xnt_flat = xn_tok.reshape([128, 3072])
        dr_sb = xth[:, 0:512]
        ur = u_s.reshape([128, 4096])
        xn3_tok = ur[:, 2048:4096].rearrange("p (t d) -> p t d", t=2)
        hs_brhs = pt_sb.reshape([128, 2048])[:, 0:SLT]
        lg_sb = xth[0:6, 0:512]
        w_fm = xth[0:6, 512:1024]

        ps1 = ctx.enter_context(nc.psum_tensor("ps1", [128, 1024], F32))
        psT = ctx.enter_context(nc.psum_tensor("psT", [128, 2048], BF16))
        ps2 = ctx.enter_context(nc.psum_tensor("ps2", [128, 2048], F32))

        class Sem:
            def __init__(self, name):
                self.h = ctx.enter_context(nc.semaphore(name))
                self.n = 0

        sT, sV, sS = Sem("sT"), Sem("sV"), Sem("sS")
        qG = Sem("qG")
        qC = Sem("qC")
        qX = Sem("qX")
        qZ = Sem("qZ")
        qW = Sem("qW")
        qO = Sem("qO")

        Q = {e: [] for e in ENGINES}

        def inc(e, sem):
            sem.n += 1
            if e in ("V", "S"):
                Q[e].append(lambda eng, sh=sem.h: eng.drain().then_inc(sh, 1))
                return sem.n
            prev = Q[e].pop()
            def wrapped(eng, prev=prev, sh=sem.h):
                r = prev(eng)
                assert r is not None, (
                    "thunk must return its last instruction: %s:%d" %
                    (prev.__code__.co_name, prev.__code__.co_firstlineno))
                r.then_inc(sh, 1)
                return r
            Q[e].append(wrapped)
            return sem.n

        def barrier():
            snap = [(sT, sT.n), (sV, sV.n), (sS, sS.n)]
            names = {id(sT): "T", id(sV): "V", id(sS): "S"}
            dsnap = [(q, q.n) for q in (qC, qX, qZ, qW, qO, qG) if q.n]
            for e in ENGINES:
                for sem, v in snap:
                    if v and names[id(sem)] != e:
                        Q[e].append(lambda eng, sh=sem.h, vv=v: eng.wait_ge(sh, vv))
                for sem, v in dsnap:
                    Q[e].append(lambda eng, sh=sem.h, vv=v: eng.wait_ge(sh, vv))

        bankfree = {}

        def wait_bank(e, key):
            ent = bankfree.get(key)
            if ent is not None:
                sem = {"V": sV, "T": sT, "S": sS}[ent[0]]
                Q[e].append(lambda eng, s=sem.h, v=ent[1]: eng.wait_ge(s, v))

        def free_bank(key, e, val):
            bankfree[key] = (e, val)

        for sbuf, src in [(bcols, dbcols), (rbcol, drbcol),
                          (ones_r, dones), (identb, didb), (identf, didf),
                          (e16, de16), (cst, dcst)]:
            qC.n += 16
            Q["Y"].append(lambda y, d=sbuf, s=src: y.dma_start(out=d[:], in_=s).then_inc(qC.h, 16))
        consts_done = qC.n

        wload = {}

        def load_w(name, dst):
            sap = dw[name].ap().rearrange("(k p) o -> p k o", p=128)
            qG.n += 16
            Q["G"].append(lambda g, d=dst, s=sap: g.dma_start(out=d, in_=s).then_inc(qG.h, 16))
            wload[name] = qG.n

        for name, dst in [("wq", wq_sb), ("wk", wk_sb), ("wv", wv_sb),
                          ("wl", wl_sb[:])]:
            load_w(name, dst)
        qG.n += 16
        Q["G"].append(lambda g: g.dma_start(out=ones_bf[:], in_=dones).then_inc(qG.h, 16))
        qG.n += 16
        Q["G"].append(lambda g: g.dma_start(out=blrow_bf[:], in_=dblrow).then_inc(qG.h, 16))
        wload_bf = qG.n

        stA = {"xst_free": 0, "fm_free_T": 0, "mmfree": {}}

        def emit_phase_A(seq):
            Q["V"].append(lambda v: v.memset(vt_sb[:, :, :, 64:65], 1.0))
            Q["V"].append(lambda v: v.memset(vt_sb[:, :, :, 129:130], 1.0))
            mmfree = stA["mmfree"]

            for sl in range(NSL):
                base = sl * SLT
                for tt in range(NTT):
                    qX.n += 16
                    def fy(y, seq=seq, base=base, tt=tt, xf=stA["xst_free"],
                           zb=stB["zdma"]):
                        y.wait_ge(sV.h, xf)
                        if zb:
                            y.wait_ge(qZ.h, zb)
                        y.dma_start(out=xst[:, tt, :],
                                    in_=dx[seq, base + tt * 128:base + (tt + 1) * 128, :]
                                    ).then_inc(qX.h, 16)
                    Q["Y"].append(fy)
                tload_all = qX.n
                for tt in range(NTT):
                    def f1(v, tt=tt, w=tload_all, cd=consts_done):
                        v.wait_ge(qX.h, w)
                        if tt == 0:
                            v.wait_ge(qC.h, cd)
                        v.drain()
                        v.tensor_scalar(dummy[:, 0:D], xst[:, tt, :], 1.0, 0.0,
                                        ALU.mult, ALU.add,
                                        accum_out=st1[:, tt:tt + 1])
                        v.drain()
                        v.scalar_tensor_tensor(dummy[:, 0:D], xst[:, tt, :], 1.0,
                                               xst[:, tt, :], ALU.mult, ALU.mult,
                                               accum_out=st2[:, tt:tt + 1])
                        v.drain()
                    Q["V"].append(f1)

                def fstat(v):
                    v.tensor_scalar(stm[:], st1[:], 1.0 / D, None, ALU.mult)
                    v.tensor_scalar(stv[:], st2[:], 1.0 / D, None, ALU.mult)
                    v.drain()
                    v.tensor_tensor(stq[:], stm[:], stm[:], ALU.mult)
                    v.drain()
                    v.tensor_sub(stv[:], stv[:], stq[:])
                    v.drain()
                    v.tensor_scalar(stv[:], stv[:], EPS, None, ALU.add)
                    v.drain()
                    return v.reciprocal(stv[:], stv[:])
                Q["V"].append(fstat)
                vstat = inc("V", sV)
                def fsq(s, v=vstat):
                    s.wait_ge(sV.h, v)
                    return s.activation(sti[:], stv[:], AF.Sqrt)
                Q["S"].append(fsq)
                sdone = inc("S", sS)

                for tt in range(NTT):
                    gt = seq * 100 + sl * NTT + tt
                    def fa(v, tt=tt, sd=sdone, tw=stA.get("xtok%d" % (tt % 2))):
                        v.wait_ge(sS.h, sd)
                        if tw is not None:
                            v.wait_ge(sT.h, tw)
                        return v.tensor_scalar(xn_tok[:, tt % 2, :], xst[:, tt, :],
                                               stm[:, tt:tt + 1], sti[:, tt:tt + 1],
                                               ALU.subtract, ALU.mult)
                    Q["V"].append(fa)
                    va = inc("V", sV)
                    bank = "p%d" % (tt % 2)
                    tp = psT[:, (tt % 2) * 1024:(tt % 2 + 1) * 1024]
                    Q["T"].append(lambda t, v=va: t.wait_ge(sV.h, v))
                    wait_bank("T", bank)
                    def ftr(t, tp=tp, tt=tt):
                        for c in range(NCH):
                            r = t.transpose(tp[:, c * 128:(c + 1) * 128],
                                            xn_tok[:, tt % 2, c * 128:(c + 1) * 128],
                                            identb[:])
                        return r
                    Q["T"].append(ftr)
                    ttr = inc("T", sT)
                    stA["xtok%d" % (tt % 2)] = ttr
                    def fev(v, tp=tp, tt=tt, w=ttr, wf=stA["fm_free_T"]):
                        v.wait_ge(sT.h, w)
                        if wf:
                            v.wait_ge(sT.h, wf)
                        return v.tensor_copy(xn_fm[:, :, tt * 128:(tt + 1) * 128],
                                             tp.rearrange("p (c t) -> p c t", c=NCH))
                    Q["V"].append(fev)
                    vev = inc("V", sV)
                    free_bank(bank, "V", vev)
                stA["xst_free"] = sV.n
                fm_ready = sV.n

                def mm_group(lhs_get, rhs_get, n_k, bank_key, pa, waits, evac, rider=None):
                    for (code, val) in waits:
                        sem = {"V": sV, "T": sT, "G": qG}[code]
                        Q["T"].append(lambda t, s=sem.h, v=val: t.wait_ge(s, v))
                    wait_bank("T", bank_key)
                    def fmm(t, n_k=n_k):
                        for ki in range(n_k):
                            r = t.matmul(pa, lhs_get(ki), rhs_get(ki),
                                         start=(ki == 0),
                                         stop=(ki == n_k - 1 and rider is None))
                        if rider is not None:
                            r = rider(t)
                        return r
                    Q["T"].append(fmm)
                    tmm = inc("T", sT)
                    def fe(v, w=tmm):
                        v.wait_ge(sT.h, w)
                        return evac(v)
                    Q["V"].append(fe)
                    vev = inc("V", sV)
                    free_bank(bank_key, "V", vev)
                    return vev

                def proj(wsb, bias0, dst_fn, waits, rhs=xn_fm):
                    last = None
                    for mo in range(NCH):
                        bk = "m%d" % (mo % 2)
                        pa = ps2[:, (mo % 2) * SLT:(mo % 2 + 1) * SLT]
                        def ev(v, pa=pa, mo=mo, bias0=bias0, dst_fn=dst_fn):
                            return v.tensor_scalar(dst_fn(mo), pa,
                                                   bcols[:, bias0 + mo:bias0 + mo + 1],
                                                   None, ALU.add)
                        last = mm_group(
                            lambda ki, mo=mo, wsb=wsb: wsb[:, ki, mo * 128:(mo + 1) * 128],
                            lambda ki, rhs=rhs: rhs[:, ki, :],
                            NCH, bk, pa, waits, ev)
                    return last

                wfull = qG.n
                uk_done = proj(wk_sb, 8, lambda mo: u_s[:, mo, :],
                               [("V", fm_ready), ("G", wfull)])
                k_done = proj(wl_sb, 24,
                              lambda mo, base=base: k_sb[:, mo, base:base + SLT],
                              [("V", uk_done), ("G", wfull)], rhs=u_s)
                t_after_k = sT.n
                uv_done = proj(wv_sb, 16, lambda mo: u_s[:, mo, :],
                               [("V", fm_ready), ("G", wfull), ("T", t_after_k)])
                for tc in range(4):
                    tkg = sl * 4 + tc
                    for jh in range(2):
                        bk = "m%d" % (jh % 2)
                        pa = ps2[:, (jh % 2) * SLT:(jh % 2 + 1) * SLT]
                        def rider(t, jh=jh, pa=pa, wb=wload_bf):
                            t.wait_ge(qG.h, wb)
                            return t.matmul(pa, ones_bf[0:1, 0:128],
                                            blrow_bf[0:1, jh * SLT:(jh + 1) * SLT],
                                            start=False, stop=True)
                        def ev(v, pa=pa, tkg=tkg, jh=jh):
                            pv = pa.rearrange("p (pr hh c) -> p pr hh c", pr=4, hh=2)
                            v.tensor_copy(vt_sb[:, tkg, jh * 4:(jh + 1) * 4, 0:64],
                                          pv[:, :, 0, :])
                            return v.tensor_copy(
                                vt_sb[:, tkg, jh * 4:(jh + 1) * 4, 65:129],
                                pv[:, :, 1, :])
                        mm_group(
                            lambda ki, tc=tc: u_s[:, ki, tc * 128:(tc + 1) * 128],
                            lambda ki, jh=jh: wl_sb[:, ki, jh * SLT:(jh + 1) * SLT],
                            NCH, bk, pa, [("V", uv_done), ("G", wfull)],
                            ev, rider=rider)
                proj(wq_sb, 0, lambda mo, base=base: q_sb[:, mo, base:base + SLT],
                     [("V", fm_ready), ("G", wfull)])
                stA["fm_free_T"] = sT.n

        stB = {"wo_done": 0, "zdma": 0, "zevT": 0}

        def emit_phase_B(seq):
            psTf = psT.bitcast(F32).reshape([128, 1024])
            stb = [ps1[:, 0:1024], psTf[:, 0:1024]]
            av2 = ps2[:, 0:1024]
            drb = [ps2[:, 1024:1536], ps2[:, 1536:2048]]
            ptfl = pt_sb.reshape([128, 2048])

            def pwide(s):
                if s < 2:
                    return ptfl[:, s * 1024:(s + 1) * 1024]
                return xnt_flat[:, (s - 1) * 1024:s * 1024]

            def pslot(s, h):
                if s < 2:
                    return pt_sb[:, s, h, :]
                o = (s - 1) * 1024 + h * SLT
                return xnt_flat[:, o:o + SLT]
            exp_done = {}
            av_done = {}
            attn_done = {}

            def emit_wo(mo, wst):
                pa = ps2[:, 1024 + (mo % 2) * 512:1024 + (mo % 2 + 1) * 512]
                def fz(t, mo=mo, pa=pa, aa=wst["aa"], wv=wst["zev"].get(mo - 2),
                       wwo=wst["wwo"]):
                    t.wait_ge(sV.h, aa)
                    t.wait_ge(qG.h, wwo)
                    if wv is not None:
                        t.wait_ge(sV.h, wv)
                    for ki in range(NCH):
                        r = t.matmul(pa, wo_sb[:, ki, mo * 128:(mo + 1) * 128],
                                     u_s[:, ki, :], start=(ki == 0), stop=(ki == 7))
                    return r
                Q["T"].append(fz)
                tz = inc("T", sT)
                if mo == NCH - 1:
                    stB["wo_done"] = tz
                def fzev(v, mo=mo, pa=pa, tz=tz, wdma=stB["zdma"]):
                    v.wait_ge(sT.h, tz)
                    if wdma:
                        v.wait_ge(qZ.h, wdma)
                    return v.tensor_scalar(z_sb[:, mo, :], pa,
                                           bcols[:, 32 + mo:32 + mo + 1],
                                           None, ALU.add)
                Q["V"].append(fzev)
                wst["zev"][mo] = inc("V", sV)
                if mo == NCH - 1:
                    stB["zevT"] = wst["zev"][mo]
                    zv = wst["zev"][mo]
                    for m2 in range(NCH):
                        qZ.n += 16
                        def fy(y, m2=m2, zv=zv, seq=wst["seq"], tqc=wst["tqc"]):
                            y.wait_ge(sV.h, zv)
                            y.dma_start(out=dzs[seq, m2, :,
                                               tqc * SLT:(tqc + 1) * SLT],
                                        in_=z_sb[:, m2, :]).then_inc(qZ.h, 16)
                        Q["Y"].append(fy)
                    stB["zdma"] = qZ.n

            pend = []
            for tqc in range(4):
                qb = tqc * SLT
                for j in range(8):
                    par = j % 2
                    avA, avB = av2[:, 0:512], av2[:, 512:1024]

                    def emit_qk(tkc):
                        pk = tkc % 2
                        pp = tkc % 4
                        wk_ = (j, tkc - 2) if tkc >= 2 else \
                            ((j - 1, 14 + tkc) if j > 0 else (7, 14 + tkc))
                        we = exp_done.get(wk_)
                        def fqk(t, j=j, tkc=tkc, qb=qb, pk=pk, we=we):
                            if we is not None:
                                t.wait_ge(sS.h, we)
                            t.matmul(stb[pk][:, 0:512],
                                     k_sb[0:64, j, tkc * 128:(tkc + 1) * 128],
                                     q_sb[0:64, j, qb:qb + SLT], start=True, stop=True,
                                     tile_position=(0, 0))
                            return t.matmul(stb[pk][:, 512:1024],
                                            k_sb[64:128, j, tkc * 128:(tkc + 1) * 128],
                                            q_sb[64:128, j, qb:qb + SLT],
                                            start=True, stop=True,
                                            tile_position=(64, 0))
                        Q["T"].append(fqk)
                        qk_done = inc("T", sT)
                        wav = av_done.get((j, tkc - 4)) if tkc >= 4 else \
                            av_done.get((j - 1, 12 + tkc) if j > 0 else (7, 12 + tkc))
                        def fexp(sc, tkc=tkc, pk=pk, pp=pp, qk=qk_done, wav=wav):
                            sc.wait_ge(sT.h, qk)
                            if wav is not None:
                                sc.wait_ge(sT.h, wav)
                            return sc.activation(pwide(pp), stb[pk][:, 0:1024],
                                                 AF.Exp, bias=cst[:, 1:2],
                                                 scale=cst[:, 0:1])
                        Q["S"].append(fexp)
                        exp_done[(j, tkc)] = inc("S", sS)

                    def emit_av(tkc):
                        pp = tkc % 4
                        ed = exp_done[(j, tkc)]
                        wat = attn_done.get(j - 1 if j >= 1 else 7)
                        def fav(t, j=j, tkc=tkc, pp=pp, ed=ed,
                                avA=avA, avB=avB, wat=wat):
                            t.wait_ge(sS.h, ed)
                            if tkc == 0 and wat is not None:
                                t.wait_ge(sV.h, wat)
                            t.matmul(avA[0:65, :], vt_sb[:, tkc, j, 0:65],
                                     pslot(pp, 0),
                                     start=(tkc == 0), stop=(tkc == 15))
                            return t.matmul(avB[0:65, :], vt_sb[:, tkc, j, 65:130],
                                            pslot(pp, 1),
                                            start=(tkc == 0), stop=(tkc == 15))
                        Q["T"].append(fav)
                        av_done[(j, tkc)] = inc("T", sT)

                    AVLAG = 2
                    for tkc in range(16 + AVLAG):
                        if tkc >= AVLAG:
                            emit_av(tkc - AVLAG)
                        if tkc < 16:
                            emit_qk(tkc)
                        if j == 0 and pend and tkc % 2 == 1:
                            emit_wo(*pend.pop(0))
                    while j == 0 and pend:
                        emit_wo(*pend.pop(0))

                    def fden(v, av2=av2, ad=av_done[(j, 15)]):
                        v.wait_ge(sT.h, ad)
                        v.tensor_copy(den_sb[0:1, :], av2[64:65, :])
                        v.drain()
                        return v.reciprocal(rden[0:1, :], den_sb[0:1, :])
                    Q["V"].append(fden)
                    den_done = inc("V", sV)
                    drp = drb[par]
                    def fdr(t, dd=den_done, drp=drp):
                        t.wait_ge(sV.h, dd)
                        t.matmul(drp[0:64, :], ones_r[0:1, 0:64], rden[0:1, 0:512],
                                 start=True, stop=True, tile_position=(0, 0))
                        return t.matmul(drp[64:128, :], ones_r[0:1, 0:64],
                                        rden[0:1, 512:1024],
                                        start=True, stop=True, tile_position=(0, 64))
                    Q["T"].append(fdr)
                    dr_done = inc("T", sT)
                    def fnorm(v, j=j, drd=dr_done, avA=avA, avB=avB, drp=drp,
                              wz=stB["wo_done"]):
                        v.wait_ge(sT.h, drd)
                        if wz:
                            t = None
                            v.wait_ge(sT.h, wz)
                        v.tensor_copy(dr_sb[:], drp)
                        v.drain()
                        v.scalar_tensor_tensor(u_s[0:64, j, :], avA[0:64, :], 1.0,
                                               dr_sb[0:64, :], ALU.mult, ALU.mult)
                        return v.scalar_tensor_tensor(u_s[64:128, j, :], avB[0:64, :],
                                                      1.0, dr_sb[64:128, :],
                                                      ALU.mult, ALU.mult)
                    Q["V"].append(fnorm)
                    attn_done[j] = inc("V", sV)

                wst = {"aa": attn_done[7], "zev": {}, "wwo": qG.n,
                       "seq": seq, "tqc": tqc}
                if tqc < 3:
                    pend = [(mo, wst) for mo in range(NCH)]
                else:
                    for mo in range(NCH):
                        emit_wo(mo, wst)

        def emit_phase_C(cw):
            ztp = [ps1[:, 0:1024], ps2[:, 1024:2048]]
            ztp_keys = [("zA",), ("zB",)]
            tp_bf = [psT[:, 0:1024], psT[:, 1024:2048]]
            ha_ps = ps2[:, 0:512]
            b7 = ps2[:, 512:1024]
            y_banks = [ps2[:, 0:512], ps2[:, 512:1024]]
            stC = {"zld_free": 0, "xt_free": 0, "x2r_free": 0, "ha_free": 0,
                   "b7_free": 0, "ysb_freeT": 0, "out_dma_all": 0}

            for s in range(NSEQ * NSL):
                seq, sl = divmod(s, NSL)
                base = sl * SLT
                gbase = seq * T + base
                for mo in range(NCH):
                    qZ.n += 16
                    def fy(y, mo=mo, seq=seq, base=base, zf=stC["zld_free"],
                           zb=stB["zdma"], xf=stC["x2r_free"]):
                        if zf:
                            y.wait_ge(sT.h, zf)
                        if zb:
                            y.wait_ge(qZ.h, zb)
                        if xf:
                            y.wait_ge(sV.h, xf)
                        y.dma_start(out=z_ld[:, mo, :],
                                    in_=dzs[seq, mo, :, base:base + SLT]
                                    ).then_inc(qZ.h, 16)
                    Q["Y"].append(fy)
                zall = qZ.n
                ztok_ev = {}
                for tt in range(NTT):
                    pk = tt % 2
                    Q["T"].append(lambda t, w=zall: t.wait_ge(qZ.h, w))
                    wait_bank("T", ztp_keys[pk][0])
                    def ftr(t, tt=tt, pk=pk):
                        for c in range(NCH):
                            r = t.transpose(ztp[pk][:, c * 128:(c + 1) * 128],
                                            z_ld[:, c, tt * 128:(tt + 1) * 128],
                                            identf[:])
                        return r
                    Q["T"].append(ftr)
                    ttr = inc("T", sT)
                    def fev(v, tt=tt, pk=pk, w=ttr):
                        v.wait_ge(sT.h, w)
                        v.drain()
                        for hh in range(2):
                            v.tensor_scalar(z_tok[:, tt, hh * 512:(hh + 1) * 512],
                                            ztp[pk][:, hh * 512:(hh + 1) * 512],
                                            1.0, 0.0, ALU.mult, ALU.add,
                                            accum_out=zsp[:, tt, hh:hh + 1])
                            r = v.drain()
                        return r
                    Q["V"].append(fev)
                    vev = inc("V", sV)
                    free_bank(ztp_keys[pk][0], "V", vev)
                    ztok_ev[tt] = vev
                stC["zld_free"] = sT.n
                def fst2(v):
                    v.drain()
                    v.reduce_sum(st1[:], zsp[:], axis=AX.X)
                    v.drain()
                    for tt in range(NTT):
                        v.scalar_tensor_tensor(dummy[:, 0:D], z_tok[:, tt, :], 1.0,
                                               z_tok[:, tt, :], ALU.mult, ALU.mult,
                                               accum_out=st2[:, tt:tt + 1])
                        v.drain()
                    v.tensor_scalar(stm[:], st1[:], 1.0 / D, None, ALU.mult)
                    v.tensor_scalar(stv[:], st2[:], 1.0 / D, None, ALU.mult)
                    v.drain()
                    v.tensor_tensor(stq[:], stm[:], stm[:], ALU.mult)
                    v.drain()
                    v.tensor_sub(stv[:], stv[:], stq[:])
                    v.drain()
                    v.tensor_scalar(stv[:], stv[:], EPS, None, ALU.add)
                    v.drain()
                    return v.reciprocal(stv[:], stv[:])
                Q["V"].append(fst2)
                v2 = inc("V", sV)
                def fsq2(s, v=v2):
                    s.wait_ge(sV.h, v)
                    return s.activation(sti[:], stv[:], AF.Sqrt)
                Q["S"].append(fsq2)
                sd2 = inc("S", sS)
                def fnm2(v, w=sd2):
                    v.wait_ge(sS.h, w)
                    v.drain()
                    v.scalar_tensor_tensor(stn[:], stm[:], -1.0, sti[:],
                                           ALU.mult, ALU.mult)
                    return v.drain()
                Q["V"].append(fnm2)
                sd2_n = inc("V", sV)
                for tt in range(NTT):
                    qX.n += 16
                    def fy(y, tt=tt, seq=seq, base=base, xf=stC["x2r_free"]):
                        if xf:
                            y.wait_ge(sV.h, xf)
                        y.dma_start(out=xt[:, tt, :],
                                    in_=dx[seq, base + tt * 128:base + (tt + 1) * 128, :]
                                    ).then_inc(qX.h, 16)
                    Q["Y"].append(fy)
                xload_all = qX.n
                x2dma = []
                for tt in range(NTT):
                    def f2a(sc, tt=tt, wn=sd2_n):
                        sc.wait_ge(sV.h, wn)
                        return sc.activation(xn2_t4[:, tt, :], z_tok[:, tt, :],
                                             AF.Identity, bias=stn[:, tt:tt + 1],
                                             scale=sti[:, tt:tt + 1])
                    Q["S"].append(f2a)
                    sx2 = inc("S", sS)
                    def f2(v, tt=tt, sx2=sx2, xl=xload_all):
                        v.wait_ge(sS.h, sx2)
                        v.wait_ge(qX.h, xl)
                        v.drain()
                        v.scalar_tensor_tensor(x2_sl[:, tt, :], xt[:, tt, :], 1.0,
                                               xn2_t4[:, tt, :], ALU.mult, ALU.add,
                                               accum_out=st1[:, tt:tt + 1])
                        v.drain()
                        v.scalar_tensor_tensor(dummy[:, 0:D], x2_sl[:, tt, :],
                                               1.0, x2_sl[:, tt, :],
                                               ALU.mult, ALU.mult,
                                               accum_out=st2[:, tt:tt + 1])
                        return v.drain()
                    Q["V"].append(f2)
                    vx2 = inc("V", sV)
                    if tt == NTT - 1:
                        stC["xt_free"] = vx2
                    qW.n += 16
                    def fy2(y, tt=tt, vx2=vx2, gbase=gbase):
                        y.wait_ge(sV.h, vx2)
                        y.dma_start(out=dx2[gbase + tt * 128:gbase + (tt + 1) * 128, :],
                                    in_=x2_sl[:, tt, :]).then_inc(qW.h, 16)
                    Q["Y"].append(fy2)
                    x2dma.append(qW.n)
                def fst3(v):
                    v.tensor_scalar(stm[:], st1[:], 1.0 / D, None, ALU.mult)
                    v.tensor_scalar(stv[:], st2[:], 1.0 / D, None, ALU.mult)
                    v.drain()
                    v.tensor_tensor(stq[:], stm[:], stm[:], ALU.mult)
                    v.drain()
                    v.tensor_sub(stv[:], stv[:], stq[:])
                    v.drain()
                    v.tensor_scalar(stv[:], stv[:], EPS, None, ALU.add)
                    v.drain()
                    return v.reciprocal(stv[:], stv[:])
                Q["V"].append(fst3)
                v3 = inc("V", sV)
                def fsq3(s, v=v3):
                    s.wait_ge(sV.h, v)
                    return s.activation(sti[:], stv[:], AF.Sqrt)
                Q["S"].append(fsq3)
                sd3 = inc("S", sS)
                def fnm3(v, w=sd3):
                    v.wait_ge(sS.h, w)
                    v.drain()
                    v.scalar_tensor_tensor(stn[:], stm[:], -1.0, sti[:],
                                           ALU.mult, ALU.mult)
                    return v.drain()
                Q["V"].append(fnm3)
                sd3_n = inc("V", sV)
                for tt in range(NTT):
                    def fa3(v, tt=tt, sd3=sd3):
                        v.wait_ge(sS.h, sd3)
                        v.drain()
                        return v.tensor_scalar(xn3_tok[:, tt % 2, :], x2_sl[:, tt, :],
                                               stm[:, tt:tt + 1], sti[:, tt:tt + 1],
                                               ALU.subtract, ALU.mult)
                    Q["V"].append(fa3)
                    va3 = inc("V", sV)
                    bank = "p%d" % (tt % 2)
                    tpb = tp_bf[tt % 2]
                    Q["T"].append(lambda t, v=va3: t.wait_ge(sV.h, v))
                    wait_bank("T", bank)
                    def ftr3(t, tpb=tpb, tt=tt):
                        for c in range(NCH):
                            r = t.transpose(tpb[:, c * 128:(c + 1) * 128],
                                            xn3_tok[:, tt % 2, c * 128:(c + 1) * 128],
                                            identb[:])
                        return r
                    Q["T"].append(ftr3)
                    ttr3 = inc("T", sT)
                    def fev3(v, tpb=tpb, tt=tt, w=ttr3, wf=stC.get("fm_free_T", 0)):
                        v.wait_ge(sT.h, w)
                        if wf:
                            v.wait_ge(sT.h, wf)
                        return v.tensor_copy(xn_fm[:, :, tt * 128:(tt + 1) * 128],
                                             tpb.rearrange("p (c t) -> p c t", c=NCH))
                    Q["V"].append(fev3)
                    vev3 = inc("V", sV)
                    free_bank(bank, "V", vev3)
                fm3_all = sV.n

                def fha(t, fm=fm3_all, wld=cw["astk"], hf=stC["ha_free"]):
                    t.wait_ge(sV.h, fm)
                    t.wait_ge(qG.h, wld)
                    if hf:
                        t.wait_ge(sV.h, hf)
                    for ki in range(NCH):
                        r = t.matmul(ha_ps, astk_a[:, ki, 0:128], xn_fm[:, ki, :],
                                     start=(ki == 0), stop=(ki == 7))
                    return r
                Q["T"].append(fha)
                tha = inc("T", sT)
                def fhl(t, hf=stC["b7_free"]):
                    if hf:
                        t.wait_ge(sV.h, hf)
                    for ki in range(NCH):
                        r = t.matmul(b7[0:6, 0:512], astk_a[:, ki, 128:134],
                                     xn_fm[:, ki, :], start=(ki == 0), stop=(ki == 7))
                    return r
                Q["T"].append(fhl)
                thl = inc("T", sT)
                def flg(v, w=thl):
                    v.wait_ge(sT.h, w)
                    return v.tensor_scalar(lg_sb, b7[0:6, 0:512], rbcol[0:6, :],
                                           None, ALU.add)
                Q["V"].append(flg)
                vlg = inc("V", sV)
                def fltr(t, w=vlg):
                    t.wait_ge(sV.h, w)
                    for tt in range(NTT):
                        r = t.transpose(b7[:, tt * 6:(tt + 1) * 6],
                                        lg_sb[:, tt * 128:(tt + 1) * 128],
                                        identf[0:6, 0:6])
                    return r
                Q["T"].append(fltr)
                tltr = inc("T", sT)
                def flgt(v, w=tltr):
                    v.wait_ge(sT.h, w)
                    return v.tensor_copy(lg_tok[:].rearrange("p t e -> p (t e)"),
                                         b7[:, 0:24])
                Q["V"].append(flgt)
                vlgt = inc("V", sV)
                def fge(sc, w=vlgt):
                    sc.wait_ge(sV.h, w)
                    return sc.activation(ge_sb[:].rearrange("p t e -> p (t e)"),
                                         lg_tok[:].rearrange("p t e -> p (t e)"), AF.Exp)
                Q["S"].append(fge)
                sge = inc("S", sS)
                def fgm(v, w=sge):
                    v.wait_ge(sS.h, w)
                    v.drain()
                    v.reduce_max(g_m1[:], ge_sb[:], axis=AX.X)
                    v.drain()
                    for e in range(6):
                        v.tensor_tensor(g_t1[:, :, e], ge_sb[:, :, e], g_m1[:],
                                        ALU.is_equal)
                        v.drain()
                    v.tensor_mul(g_t2[:], ge_sb[:], g_t1[:])
                    v.drain()
                    v.tensor_sub(g_t2[:], ge_sb[:], g_t2[:])
                    v.drain()
                    v.reduce_max(g_m2[:], g_t2[:], axis=AX.X)
                    v.drain()
                    v.reduce_sum(g_s[:], ge_sb[:], axis=AX.X)
                    v.drain()
                    v.tensor_add(g_rd[:], g_m1[:], g_m2[:])
                    v.drain()
                    v.scalar_tensor_tensor(g_rd[:], g_s[:], 1e-6, g_rd[:],
                                           ALU.mult, ALU.add)
                    v.drain()
                    v.reciprocal(g_rd[:], g_rd[:])
                    v.drain()
                    for e in range(6):
                        v.tensor_tensor(g_t1[:, :, e], ge_sb[:, :, e], g_m2[:],
                                        ALU.is_ge)
                        v.drain()
                    v.tensor_mul(g_t2[:], ge_sb[:], g_t1[:])
                    v.drain()
                    for e in range(6):
                        r = v.tensor_tensor(w_tok[:, :, e], g_t2[:, :, e], g_rd[:],
                                            ALU.mult)
                    return r
                Q["V"].append(fgm)
                vw = inc("V", sV)
                def fwtr(t, w=vw):
                    t.wait_ge(sV.h, w)
                    for tt in range(NTT):
                        r = t.transpose(b7[0:6, tt * 128:(tt + 1) * 128],
                                        w_tok[:, tt, :], identf[:])
                    return r
                Q["T"].append(fwtr)
                twtr = inc("T", sT)
                def fwfm(v, w=twtr):
                    v.wait_ge(sT.h, w)
                    return v.tensor_copy(w_fm, b7[0:6, 0:512])
                Q["V"].append(fwfm)
                vwfm = inc("V", sV)
                def fwrep(t, w=vwfm):
                    t.wait_ge(sV.h, w)
                    return t.matmul(b7, e16[:], w_fm, start=True, stop=True)
                Q["T"].append(fwrep)
                twrep = inc("T", sT)
                def fbr(v, w1=tha, w2=twrep):
                    v.wait_ge(sT.h, w1)
                    v.wait_ge(sT.h, w2)
                    v.tensor_copy(hs_brhs, ha_ps)
                    v.drain()
                    v.tensor_tensor(hs_brhs[32:64, :], hs_brhs[32:64, :],
                                    b7[32:64, :], ALU.mult)
                    return v.tensor_tensor(hs_brhs[64:128, :], hs_brhs[64:128, :],
                                            b7[64:128, :], ALU.mult)
                Q["V"].append(fbr)
                vbr = inc("V", sV)
                stC["ha_free"] = vbr
                stC["b7_free"] = vbr
                yev = {}
                for mo in range(NCH):
                    pa = y_banks[mo % 2]
                    def fy(t, mo=mo, pa=pa, vbr=vbr, fm=fm3_all,
                           wv=yev.get(mo - 2), wl1=cw["bwT"], wl2=cw["bstk"],
                           wts=stC["ysb_freeT"]):
                        t.wait_ge(sV.h, vbr)
                        t.wait_ge(sV.h, fm)
                        t.wait_ge(qG.h, max(wl1, wl2))
                        if wv is not None:
                            t.wait_ge(sV.h, wv)
                        for ki in range(NCH):
                            t.matmul(pa, bw_sb[:, ki, mo * 128:(mo + 1) * 128],
                                     xn_fm[:, ki, :], start=(ki == 0), stop=False)
                        return t.matmul(pa, bstk_a[:, mo, 134:262], hs_brhs,
                                        start=False, stop=True)
                    Q["T"].append(fy)
                    ty = inc("T", sT)
                    def fyev(v, mo=mo, pa=pa, w=ty, wts=stC["ysb_freeT"]):
                        v.wait_ge(sT.h, w)
                        if wts:
                            v.wait_ge(sT.h, wts)
                        return v.tensor_scalar(y_sb[:, mo, :], pa,
                                               bcols[:, 40 + mo:40 + mo + 1],
                                               None, ALU.add)
                    Q["V"].append(fyev)
                    yev[mo] = inc("V", sV)
                y_all = yev[NCH - 1]
                for tt in range(NTT):
                    pk = tt % 2
                    Q["T"].append(lambda t, w=y_all: t.wait_ge(sV.h, w))
                    wait_bank("T", ztp_keys[pk][0])
                    def ftr4(t, tt=tt, pk=pk):
                        for c in range(NCH):
                            r = t.transpose(ztp[pk][:, c * 128:(c + 1) * 128],
                                            y_sb[:, c, tt * 128:(tt + 1) * 128],
                                            identf[:])
                        return r
                    Q["T"].append(ftr4)
                    ttr4 = inc("T", sT)
                    def fev4(v, tt=tt, pk=pk, w=ttr4):
                        v.wait_ge(sT.h, w)
                        v.drain()
                        for hh in range(2):
                            v.tensor_scalar(y_tok[:, tt, hh * 512:(hh + 1) * 512],
                                            ztp[pk][:, hh * 512:(hh + 1) * 512],
                                            1.0, 0.0, ALU.mult, ALU.add,
                                            accum_out=zsp[:, tt, hh:hh + 1])
                            r = v.drain()
                        return r
                    Q["V"].append(fev4)
                    vev4 = inc("V", sV)
                    free_bank(ztp_keys[pk][0], "V", vev4)
                stC["ysb_freeT"] = sT.n
                stC["fm_free_T"] = sT.n
                def fst4(v):
                    v.drain()
                    v.reduce_sum(st1[:], zsp[:], axis=AX.X)
                    v.drain()
                    for tt in range(NTT):
                        v.scalar_tensor_tensor(dummy[:, 0:D], y_tok[:, tt, :], 1.0,
                                               y_tok[:, tt, :], ALU.mult, ALU.mult,
                                               accum_out=st2[:, tt:tt + 1])
                        v.drain()
                    v.tensor_scalar(stm[:], st1[:], 1.0 / D, None, ALU.mult)
                    v.tensor_scalar(stv[:], st2[:], 1.0 / D, None, ALU.mult)
                    v.drain()
                    v.tensor_tensor(stq[:], stm[:], stm[:], ALU.mult)
                    v.drain()
                    v.tensor_sub(stv[:], stv[:], stq[:])
                    v.drain()
                    v.tensor_scalar(stv[:], stv[:], EPS, None, ALU.add)
                    v.drain()
                    return v.reciprocal(stv[:], stv[:])
                Q["V"].append(fst4)
                v4 = inc("V", sV)
                def fsq4(s, v=v4):
                    s.wait_ge(sV.h, v)
                    return s.activation(sti[:], stv[:], AF.Sqrt)
                Q["S"].append(fsq4)
                sd4 = inc("S", sS)
                def fnm4(v, w=sd4):
                    v.wait_ge(sS.h, w)
                    v.drain()
                    v.scalar_tensor_tensor(stn[:], stm[:], -1.0, sti[:],
                                           ALU.mult, ALU.mult)
                    return v.drain()
                Q["V"].append(fnm4)
                sd4_n = inc("V", sV)
                for tt in range(NTT):
                    qX.n += 16
                    def fyx(y, tt=tt, gbase=gbase, ww=x2dma[-1], xf=stC["xt_free"]):
                        y.wait_ge(qW.h, ww)
                        if xf:
                            y.wait_ge(sV.h, xf)
                        y.dma_start(out=x2r[:, tt, :],
                                    in_=dx2[gbase + tt * 128:gbase + (tt + 1) * 128, :]
                                    ).then_inc(qX.h, 16)
                    Q["Y"].append(fyx)
                x2l_all = qX.n
                for tt in range(NTT):
                    def ffin(v, tt=tt, sd4=sd4, xl=x2l_all,
                             wod=stC["out_dma_all"]):
                        v.wait_ge(sS.h, sd4)
                        v.wait_ge(qX.h, xl)
                        if wod:
                            v.wait_ge(qO.h, wod)
                        v.drain()
                        v.tensor_scalar(xn2_t4[:, tt, :], y_tok[:, tt, :],
                                        stm[:, tt:tt + 1], sti[:, tt:tt + 1],
                                        ALU.subtract, ALU.mult)
                        v.drain()
                        return v.tensor_add(out_sb[:, tt, :], x2r[:, tt, :],
                                            xn2_t4[:, tt, :])
                    Q["V"].append(ffin)
                    vfin = inc("V", sV)
                    if tt == NTT - 1:
                        stC["x2r_free"] = vfin
                    qO.n += 16
                    def fyo(y, tt=tt, vfin=vfin, seq=seq, base=base):
                        y.wait_ge(sV.h, vfin)
                        y.dma_start(out=dout[seq, base + tt * 128:base + (tt + 1) * 128, :],
                                    in_=out_sb[:, tt, :]).then_inc(qO.h, 16)
                    Q["Y"].append(fyo)
                stC["out_dma_all"] = qO.n

        phase_list = {"A": "A0", "AB": "A0 B0", "ABA": "A0 B0 A1",
                      "ABAB": "A0 B0 A1 B1",
                      "FULL": "A0 B0 A1 B1 C"}[KPHASES].split()
        for ph in phase_list:
            if ph[0] == "A":
                seq = int(ph[1])
                if seq > 0:
                    load_w("wq", wq_sb)
                emit_phase_A(seq)
                barrier()
                if KDBG and seq == 0:
                    for dst, src in [(ddbg_q, q_sb[:]), (ddbg_k, k_sb[:]),
                                     (ddbg_v, vt_sb[:]), (ddbg_w, wq_sb),
                                     (ddbg_x, xn_fm[:]), (ddbg_t, xn_tok[:]),
                                     (ddbg_s, None)]:
                        if src is None:
                            for i2, tsr in enumerate((st1, st2, stm, stv, sti)):
                                qO.n += 16
                                Q["Y"].append(lambda y, d=ddbg_s[:, i2, :], s2=tsr:
                                              y.dma_start(out=d, in_=s2[:]).then_inc(qO.h, 16))
                            continue
                        qO.n += 16
                        Q["Y"].append(lambda y, d=dst, s=src:
                                      y.dma_start(out=d, in_=s).then_inc(qO.h, 16))
                    barrier()
            elif ph[0] == "B":
                seq = int(ph[1])
                load_w("wo", wo_sb)
                barrier()
                emit_phase_B(seq)
                barrier()
                if KDBG and seq == 0:
                    qO.n += 16
                    Q["Y"].append(lambda y: y.dma_start(out=ddbg_a, in_=u_s[:]
                                                        ).then_inc(qO.h, 16))
                    qO.n += 16
                    Q["Y"].append(lambda y: y.dma_start(out=ddbg_z0, in_=dzs[0]
                                                        ).then_inc(qO.h, 16))
                    barrier()

        cw = {}
        do_c = "C" in phase_list
        for name, dstap, srcap in ([] if not do_c else [
            ("bwT", bw_sb, dw["bwT"].ap().rearrange("(k p) o -> p k o", p=128)),
            ("astk", astk_a[:, :, 0:134],
             dastk.ap().rearrange("(k p) o -> p k o", p=128)),
            ("bstk", bstk_a[:, :, 134:262],
             dbstk.ap().rearrange("p (m c) -> p m c", m=NCH)),
        ]):
            qG.n += 16
            Q["G"].append(lambda g, d=dstap, s=srcap: g.dma_start(out=d, in_=s).then_inc(qG.h, 16))
            cw[name] = qG.n
        if do_c:
            barrier()
            emit_phase_C(cw)

        if KDBG and "C" in phase_list:
            barrier()
            for d_, s_ in [(ddbg_l, lg_tok[:]), (ddbg_g, w_tok[:]), (ddbg_e, ge_sb[:])]:
                qO.n += 16
                Q["Y"].append(lambda y, d=d_, s2=s_: y.dma_start(out=d, in_=s2).then_inc(qO.h, 16))
        totals = [(qC, qC.n), (qX, qX.n), (qZ, qZ.n), (qW, qW.n),
                  (qO, qO.n), (qG, qG.n)]
        def ffinal(g, totals=tuple((t[0].h, t[1]) for t in totals)):
            for h, v in totals:
                if v:
                    g.wait_ge(h, v)
        Q["G"].append(ffinal)

        block = ctx.enter_context(nc.Block())

        @block.tensor
        def _(t):
            for f in Q["T"]:
                f(t)

        @block.vector
        def _(v):
            for f in Q["V"]:
                f(v)

        @block.scalar
        def _(s):
            for f in Q["S"]:
                f(s)

        @block.gpsimd
        def _(g):
            for f in Q["G"]:
                f(g)

        @block.sync
        def _(y):
            for f in Q["Y"]:
                f(y)

    return nc


def _prep(inp):
    f32 = lambda a: np.ascontiguousarray(np.asarray(a), dtype=np.float32)
    x = f32(inp["L0_attn_x"])
    Wq, bq = f32(inp["L0_attn_Wq"]), f32(inp["L0_attn_bq"])
    Wk, bk = f32(inp["L0_attn_Wk"]), f32(inp["L0_attn_bk"])
    Wv, bv = f32(inp["L0_attn_Wv"]), f32(inp["L0_attn_bv"])
    Wl, bl = f32(inp["L0_attn_Wl"]), f32(inp["L0_attn_bl"])
    Wo, bo = f32(inp["L0_attn_Wo"]), f32(inp["L0_attn_bo"])
    bW, bb = f32(inp["L0_ffn_base_W"]), f32(inp["L0_ffn_base_b"])
    shA, shB = f32(inp["L0_ffn_shared_A"]), f32(inp["L0_ffn_shared_B"])
    rA, rB = f32(inp["L0_ffn_routed_A"]), f32(inp["L0_ffn_routed_B"])
    rW, rb = f32(inp["L0_ffn_router_W"]), f32(inp["L0_ffn_router_b"])

    astk = np.zeros((D, 134), np.float32)
    astk[:, 0:16] = shA.T
    for e in range(6):
        astk[:, 32 + 16 * e:48 + 16 * e] = rA[e].T
    astk[:, 128:134] = rW.T
    bstk = np.zeros((128, D), np.float32)
    bstk[0:16] = shB.T
    for e in range(6):
        bstk[32 + 16 * e:48 + 16 * e] = rB[e].T
    bcols = np.zeros((128, 48), np.float32)
    for i, b in enumerate([bq, bk, bv, bl, bo, bb]):
        bcols[:, 8 * i:8 * (i + 1)] = b.reshape(8, 128).T
    rbcol = np.zeros((128, 1), np.float32)
    rbcol[0:6, 0] = rb
    e16aug = np.zeros((6, 128), np.float32)
    for e in range(6):
        e16aug[e, 32 + 16 * e:48 + 16 * e] = 1.0
    consts = np.zeros((128, 4), np.float32)
    consts[:, 0] = 0.125
    consts[:, 1] = -SHIFT
    common = dict(
        wq=Wq, wk=Wk, wv=Wv, wl=Wl, wo=Wo, bwT=np.ascontiguousarray(bW.T),
        astk=astk, bstk=bstk, bcols=bcols, rbcol=rbcol,
        blrow=bl.reshape(1, D), onesr=np.ones((1, 128), np.float32),
        identbf=np.eye(128, dtype=ml_dtypes.bfloat16),
        identf=np.eye(128, dtype=np.float32),
        e16aug=e16aug, consts=consts,
    )
    return x, common


def kernel(**inp):
    x, common = _prep(inp)
    nc = build_nc()
    in_maps = [dict(common, x=np.ascontiguousarray(x[2 * c:2 * c + 2]))
               for c in range(8)]
    res = run_bass_kernel_spmd(nc, in_maps, core_ids=list(range(8)))
    out = np.concatenate([r["out"] for r in res.results], axis=0)
    return out.reshape(16, T, D)


# version 11
# speedup vs baseline: 1.0387x; 1.0387x over previous
import numpy as np
import ml_dtypes
from contextlib import ExitStack

import os
import concourse.bass as bass
import concourse.mybir as mybir
from concourse.bass_utils import run_bass_kernel_spmd

KPHASES = os.environ.get("KPHASES", "FULL")
KDBG = bool(int(os.environ.get("KDBG", "0")))

F32 = mybir.dt.float32
BF16 = mybir.dt.bfloat16
AF = mybir.ActivationFunctionType
ALU = mybir.AluOpType
AX = mybir.AxisListType

NSEQ = 2
T = 2048
D = 1024
NCH = 8
NSL = 4
SLT = 512
NTT = 4
EPS = 1e-5
SHIFT = 3.0

ENGINES = ("T", "V", "S", "G", "Y")


def build_nc():
    nc = bass.Bass("TRN2", target_bir_lowering=False, detect_race_conditions=False)

    dx = nc.dram_tensor("x", [NSEQ, T, D], F32, kind="ExternalInput").ap()
    dw = {n: nc.dram_tensor(n, [D, D], F32, kind="ExternalInput")
          for n in ("wq", "wk", "wv", "wl", "wo", "bwT")}
    dastk = nc.dram_tensor("astk", [D, 134], F32, kind="ExternalInput")
    dbstk = nc.dram_tensor("bstk", [128, D], F32, kind="ExternalInput")
    dbcols = nc.dram_tensor("bcols", [128, 48], F32, kind="ExternalInput").ap()
    drbcol = nc.dram_tensor("rbcol", [128, 1], F32, kind="ExternalInput").ap()
    dblrow = nc.dram_tensor("blrow", [1, D], F32, kind="ExternalInput").ap()
    dones = nc.dram_tensor("onesr", [1, 128], F32, kind="ExternalInput").ap()
    didb = nc.dram_tensor("identbf", [128, 128], BF16, kind="ExternalInput").ap()
    didf = nc.dram_tensor("identf", [128, 128], F32, kind="ExternalInput").ap()
    de16 = nc.dram_tensor("e16aug", [6, 128], F32, kind="ExternalInput").ap()
    dcst = nc.dram_tensor("consts", [128, 4], F32, kind="ExternalInput").ap()
    scratch_kind = "ExternalOutput" if KDBG else "Internal"
    dzs = nc.dram_tensor("z_scr", [NSEQ, NCH, 128, T], F32, kind=scratch_kind).ap()
    dx2 = nc.dram_tensor("x2_scr", [NSEQ * T, D], F32, kind=scratch_kind).ap()
    dout = nc.dram_tensor("out", [NSEQ, T, D], F32, kind="ExternalOutput").ap()
    if KDBG:
        ddbg_q = nc.dram_tensor("dbg_q", [128, NCH, T], BF16, kind="ExternalOutput").ap()
        ddbg_k = nc.dram_tensor("dbg_k", [128, NCH, T], BF16, kind="ExternalOutput").ap()
        ddbg_v = nc.dram_tensor("dbg_v", [128, 16, 8, 130], BF16, kind="ExternalOutput").ap()
        ddbg_a = nc.dram_tensor("dbg_a", [128, NCH, SLT], BF16, kind="ExternalOutput").ap()
        ddbg_w = nc.dram_tensor("dbg_w", [128, NCH, D], BF16, kind="ExternalOutput").ap()
        ddbg_x = nc.dram_tensor("dbg_x", [128, NCH, SLT], BF16, kind="ExternalOutput").ap()
        ddbg_t = nc.dram_tensor("dbg_t", [128, 2, D], BF16, kind="ExternalOutput").ap()
        ddbg_s = nc.dram_tensor("dbg_s", [128, 5, NTT], F32, kind="ExternalOutput").ap()
        ddbg_z0 = nc.dram_tensor("dbg_z0", [NCH, 128, T], F32, kind="ExternalOutput").ap()
        ddbg_l = nc.dram_tensor("dbg_l", [128, NTT, 6], F32, kind="ExternalOutput").ap()
        ddbg_g = nc.dram_tensor("dbg_g", [128, NTT, 6], F32, kind="ExternalOutput").ap()
        ddbg_e = nc.dram_tensor("dbg_e", [128, NTT, 6], F32, kind="ExternalOutput").ap()

    with ExitStack() as ctx:
        def sb(name, shape, dtype, **kw):
            return ctx.enter_context(nc.sbuf_tensor(name, shape, dtype, **kw))

        arena1 = sb("arena1", [128, 3, NCH, D], BF16)
        wl_sb = sb("wl_sb", [128, NCH, D], BF16)
        wq_sb, wk_sb, wv_sb = arena1[:, 0], arena1[:, 1], arena1[:, 2]
        wo_sb = arena1[:, 0]
        bw_sb = wl_sb[:]
        astk_a = arena1[:, 1]
        bstk_a = arena1[:, 2]

        bcols = sb("bcols_sb", [128, 48], F32)
        rbcol = sb("rbcol_sb", [128, 1], F32)
        ones_r = sb("ones_sb", [1, 128], F32)
        identb = sb("identb_sb", [128, 128], BF16)
        ones_bf = sb("ones_bf", [1, 128], BF16)
        blrow_bf = sb("blrow_bf", [1, D], BF16)
        identf = sb("identf_sb", [128, 128], F32)
        e16 = sb("e16_sb", [6, 128], F32)
        cst = sb("cst_sb", [128, 4], F32)

        xst = sb("xst", [128, NTT, D], F32)
        st1 = sb("st1", [128, NTT], F32)
        st2 = sb("st2", [128, NTT], F32)
        stm = sb("stm", [128, NTT], F32)
        stv = sb("stv", [128, NTT], F32)
        sti = sb("sti", [128, NTT], F32)
        stq = sb("stq", [128, NTT], F32)
        stn = sb("stn", [128, NTT], F32)
        xn_tok = sb("xn_tok", [128, 3, D], BF16)
        xn_fm = sb("xn_fm", [128, NCH, SLT], BF16)
        u_s = sb("u_s", [128, NCH, SLT], BF16)
        q_sb = sb("q_sb", [128, NCH, T], BF16)
        k_sb = sb("k_sb", [128, NCH, T], BF16)
        vt_sb = sb("vt_sb", [128, 16, 8, 130], BF16)
        pt_sb = sb("pt_sb", [128, 2, 2, SLT], BF16)
        pth = pt_sb.bitcast(F32).reshape([128, 1024])
        dummy = pth[:, 0:1024]
        zsp = sb("zsp", [128, NTT, 2], F32)
        lg_tok = sb("lg_tok", [128, NTT, 6], F32)
        ge_sb = sb("ge_sb", [128, NTT, 6], F32)
        g_t1 = sb("g_t1", [128, NTT, 6], F32)
        g_t2 = sb("g_t2", [128, NTT, 6], F32)
        g_m1 = sb("g_m1", [128, NTT], F32)
        g_m2 = sb("g_m2", [128, NTT], F32)
        g_s = sb("g_s", [128, NTT], F32)
        g_rd = sb("g_rd", [128, NTT], F32)
        w_tok = sb("w_tok", [128, NTT, 6], F32)

        z_sb = xst[:].rearrange("p a b -> p (a b)").rearrange(
            "p (m c) -> p m c", m=NCH)
        z_ld = z_sb
        kh = k_sb.bitcast(F32).reshape([128, 8192])
        z_tok = kh[:, 0:4096].rearrange("p (t d) -> p t d", t=NTT)
        y_tok = z_tok
        xt = kh[:, 4096:8192].rearrange("p (t d) -> p t d", t=NTT)
        x2r = xt
        qh = q_sb.bitcast(F32).reshape([128, 8192])
        x2_sl = qh[:, 0:4096].rearrange("p (t d) -> p t d", t=NTT)
        out_sb = qh[:, 4096:8192].rearrange("p (t d) -> p t d", t=NTT)
        vh = vt_sb.bitcast(F32).reshape([128, 8320])
        y_sb = vh[:, 0:4096].rearrange("p (m c) -> p m c", m=NCH)
        uh = u_s.bitcast(F32).reshape([128, 2048])
        xn2_t = uh[:, 0:1024]
        xn2_t4 = xst[:]
        fmh = xn_fm.bitcast(F32).reshape([128, 2048])
        den_sb = fmh[0:1, 0:1024]
        rden = fmh[0:1, 1024:2048]
        xth = xn_tok.bitcast(F32).reshape([128, 1536])
        xnt_flat = xn_tok.reshape([128, 3072])
        dr_sb = xth[:, 0:512]
        ur = u_s.reshape([128, 4096])
        xn3_tok = ur[:, 2048:4096].rearrange("p (t d) -> p t d", t=2)
        hs_brhs = pt_sb.reshape([128, 2048])[:, 0:SLT]
        lg_sb = xth[0:6, 0:512]
        w_fm = xth[0:6, 512:1024]

        ps1 = ctx.enter_context(nc.psum_tensor("ps1", [128, 1024], F32))
        psT = ctx.enter_context(nc.psum_tensor("psT", [128, 2048], BF16))
        ps2 = ctx.enter_context(nc.psum_tensor("ps2", [128, 2048], F32))

        class Sem:
            def __init__(self, name):
                self.h = ctx.enter_context(nc.semaphore(name))
                self.n = 0

        sT, sV, sS = Sem("sT"), Sem("sV"), Sem("sS")
        qG = Sem("qG")
        qC = Sem("qC")
        qX = Sem("qX")
        qZ = Sem("qZ")
        qW = Sem("qW")
        qO = Sem("qO")

        Q = {e: [] for e in ENGINES}

        def inc(e, sem):
            sem.n += 1
            if e in ("V", "S"):
                Q[e].append(lambda eng, sh=sem.h: eng.drain().then_inc(sh, 1))
                return sem.n
            prev = Q[e].pop()
            def wrapped(eng, prev=prev, sh=sem.h):
                r = prev(eng)
                assert r is not None, (
                    "thunk must return its last instruction: %s:%d" %
                    (prev.__code__.co_name, prev.__code__.co_firstlineno))
                r.then_inc(sh, 1)
                return r
            Q[e].append(wrapped)
            return sem.n

        def barrier():
            snap = [(sT, sT.n), (sV, sV.n), (sS, sS.n)]
            names = {id(sT): "T", id(sV): "V", id(sS): "S"}
            dsnap = [(q, q.n) for q in (qC, qX, qZ, qW, qO, qG) if q.n]
            for e in ENGINES:
                for sem, v in snap:
                    if v and names[id(sem)] != e:
                        Q[e].append(lambda eng, sh=sem.h, vv=v: eng.wait_ge(sh, vv))
                for sem, v in dsnap:
                    Q[e].append(lambda eng, sh=sem.h, vv=v: eng.wait_ge(sh, vv))

        bankfree = {}

        def wait_bank(e, key):
            ent = bankfree.get(key)
            if ent is not None:
                sem = {"V": sV, "T": sT, "S": sS}[ent[0]]
                Q[e].append(lambda eng, s=sem.h, v=ent[1]: eng.wait_ge(s, v))

        def free_bank(key, e, val):
            bankfree[key] = (e, val)

        for sbuf, src in [(bcols, dbcols), (rbcol, drbcol),
                          (ones_r, dones), (identb, didb), (identf, didf),
                          (e16, de16), (cst, dcst)]:
            qC.n += 16
            Q["Y"].append(lambda y, d=sbuf, s=src: y.dma_start(out=d[:], in_=s).then_inc(qC.h, 16))
        consts_done = qC.n

        wload = {}

        def load_w(name, dst):
            sap = dw[name].ap().rearrange("(k p) o -> p k o", p=128)
            qG.n += 16
            Q["G"].append(lambda g, d=dst, s=sap: g.dma_start(out=d, in_=s).then_inc(qG.h, 16))
            wload[name] = qG.n

        for name, dst in [("wq", wq_sb), ("wk", wk_sb), ("wv", wv_sb),
                          ("wl", wl_sb[:])]:
            load_w(name, dst)
        qG.n += 16
        Q["G"].append(lambda g: g.dma_start(out=ones_bf[:], in_=dones).then_inc(qG.h, 16))
        qG.n += 16
        Q["G"].append(lambda g: g.dma_start(out=blrow_bf[:], in_=dblrow).then_inc(qG.h, 16))
        wload_bf = qG.n

        stA = {"xst_free": 0, "fm_free_T": 0, "mmfree": {}}

        def emit_phase_A(seq):
            Q["V"].append(lambda v: v.memset(vt_sb[:, :, :, 64:65], 1.0))
            Q["V"].append(lambda v: v.memset(vt_sb[:, :, :, 129:130], 1.0))
            mmfree = stA["mmfree"]

            for sl in range(NSL):
                base = sl * SLT
                for tt in range(NTT):
                    qX.n += 16
                    def fy(y, seq=seq, base=base, tt=tt, xf=stA["xst_free"],
                           zb=stB["zdma"]):
                        y.wait_ge(sV.h, xf)
                        if zb:
                            y.wait_ge(qZ.h, zb)
                        y.dma_start(out=xst[:, tt, :],
                                    in_=dx[seq, base + tt * 128:base + (tt + 1) * 128, :]
                                    ).then_inc(qX.h, 16)
                    Q["Y"].append(fy)
                tload_all = qX.n
                for tt in range(NTT):
                    def f1(v, tt=tt, w=tload_all, cd=consts_done):
                        v.wait_ge(qX.h, w)
                        if tt == 0:
                            v.wait_ge(qC.h, cd)
                        v.drain()
                        v.tensor_scalar(dummy[:, 0:D], xst[:, tt, :], 1.0, 0.0,
                                        ALU.mult, ALU.add,
                                        accum_out=st1[:, tt:tt + 1])
                        v.drain()
                        v.scalar_tensor_tensor(dummy[:, 0:D], xst[:, tt, :], 1.0,
                                               xst[:, tt, :], ALU.mult, ALU.mult,
                                               accum_out=st2[:, tt:tt + 1])
                        v.drain()
                    Q["V"].append(f1)

                def fstat(v):
                    v.tensor_scalar(stm[:], st1[:], 1.0 / D, None, ALU.mult)
                    v.tensor_scalar(stv[:], st2[:], 1.0 / D, None, ALU.mult)
                    v.drain()
                    v.tensor_tensor(stq[:], stm[:], stm[:], ALU.mult)
                    v.drain()
                    v.tensor_sub(stv[:], stv[:], stq[:])
                    v.drain()
                    v.tensor_scalar(stv[:], stv[:], EPS, None, ALU.add)
                    v.drain()
                    return v.reciprocal(stv[:], stv[:])
                Q["V"].append(fstat)
                vstat = inc("V", sV)
                def fsq(s, v=vstat):
                    s.wait_ge(sV.h, v)
                    return s.activation(sti[:], stv[:], AF.Sqrt)
                Q["S"].append(fsq)
                sdone = inc("S", sS)

                for tt in range(NTT):
                    gt = seq * 100 + sl * NTT + tt
                    def fa(v, tt=tt, sd=sdone, tw=stA.get("xtok%d" % (tt % 2))):
                        v.wait_ge(sS.h, sd)
                        if tw is not None:
                            v.wait_ge(sT.h, tw)
                        return v.tensor_scalar(xn_tok[:, tt % 2, :], xst[:, tt, :],
                                               stm[:, tt:tt + 1], sti[:, tt:tt + 1],
                                               ALU.subtract, ALU.mult)
                    Q["V"].append(fa)
                    va = inc("V", sV)
                    bank = "p%d" % (tt % 2)
                    tp = psT[:, (tt % 2) * 1024:(tt % 2 + 1) * 1024]
                    Q["T"].append(lambda t, v=va: t.wait_ge(sV.h, v))
                    wait_bank("T", bank)
                    def ftr(t, tp=tp, tt=tt):
                        for c in range(NCH):
                            r = t.transpose(tp[:, c * 128:(c + 1) * 128],
                                            xn_tok[:, tt % 2, c * 128:(c + 1) * 128],
                                            identb[:])
                        return r
                    Q["T"].append(ftr)
                    ttr = inc("T", sT)
                    stA["xtok%d" % (tt % 2)] = ttr
                    def fev(v, tp=tp, tt=tt, w=ttr, wf=stA["fm_free_T"]):
                        v.wait_ge(sT.h, w)
                        if wf:
                            v.wait_ge(sT.h, wf)
                        return v.tensor_copy(xn_fm[:, :, tt * 128:(tt + 1) * 128],
                                             tp.rearrange("p (c t) -> p c t", c=NCH))
                    Q["V"].append(fev)
                    vev = inc("V", sV)
                    free_bank(bank, "V", vev)
                stA["xst_free"] = sV.n
                fm_ready = sV.n

                def mm_group(lhs_get, rhs_get, n_k, bank_key, pa, waits, evac, rider=None):
                    for (code, val) in waits:
                        sem = {"V": sV, "T": sT, "G": qG}[code]
                        Q["T"].append(lambda t, s=sem.h, v=val: t.wait_ge(s, v))
                    wait_bank("T", bank_key)
                    def fmm(t, n_k=n_k):
                        for ki in range(n_k):
                            r = t.matmul(pa, lhs_get(ki), rhs_get(ki),
                                         start=(ki == 0),
                                         stop=(ki == n_k - 1 and rider is None))
                        if rider is not None:
                            r = rider(t)
                        return r
                    Q["T"].append(fmm)
                    tmm = inc("T", sT)
                    def fe(v, w=tmm):
                        v.wait_ge(sT.h, w)
                        return evac(v)
                    Q["V"].append(fe)
                    vev = inc("V", sV)
                    free_bank(bank_key, "V", vev)
                    return vev

                def proj(wsb, bias0, dst_fn, waits, rhs=xn_fm):
                    last = None
                    for mo in range(NCH):
                        bk = "m%d" % (mo % 2)
                        pa = ps2[:, (mo % 2) * SLT:(mo % 2 + 1) * SLT]
                        def ev(v, pa=pa, mo=mo, bias0=bias0, dst_fn=dst_fn):
                            return v.tensor_scalar(dst_fn(mo), pa,
                                                   bcols[:, bias0 + mo:bias0 + mo + 1],
                                                   None, ALU.add)
                        last = mm_group(
                            lambda ki, mo=mo, wsb=wsb: wsb[:, ki, mo * 128:(mo + 1) * 128],
                            lambda ki, rhs=rhs: rhs[:, ki, :],
                            NCH, bk, pa, waits, ev)
                    return last

                wfull = qG.n
                uk_done = proj(wk_sb, 8, lambda mo: u_s[:, mo, :],
                               [("V", fm_ready), ("G", wfull)])
                k_done = proj(wl_sb, 24,
                              lambda mo, base=base: k_sb[:, mo, base:base + SLT],
                              [("V", uk_done), ("G", wfull)], rhs=u_s)
                t_after_k = sT.n
                uv_done = proj(wv_sb, 16, lambda mo: u_s[:, mo, :],
                               [("V", fm_ready), ("G", wfull), ("T", t_after_k)])
                for tc in range(4):
                    tkg = sl * 4 + tc
                    for jh in range(2):
                        bk = "m%d" % (jh % 2)
                        pa = ps2[:, (jh % 2) * SLT:(jh % 2 + 1) * SLT]
                        def rider(t, jh=jh, pa=pa, wb=wload_bf):
                            t.wait_ge(qG.h, wb)
                            return t.matmul(pa, ones_bf[0:1, 0:128],
                                            blrow_bf[0:1, jh * SLT:(jh + 1) * SLT],
                                            start=False, stop=True)
                        def ev(v, pa=pa, tkg=tkg, jh=jh):
                            pv = pa.rearrange("p (pr hh c) -> p pr hh c", pr=4, hh=2)
                            v.tensor_copy(vt_sb[:, tkg, jh * 4:(jh + 1) * 4, 0:64],
                                          pv[:, :, 0, :])
                            return v.tensor_copy(
                                vt_sb[:, tkg, jh * 4:(jh + 1) * 4, 65:129],
                                pv[:, :, 1, :])
                        mm_group(
                            lambda ki, tc=tc: u_s[:, ki, tc * 128:(tc + 1) * 128],
                            lambda ki, jh=jh: wl_sb[:, ki, jh * SLT:(jh + 1) * SLT],
                            NCH, bk, pa, [("V", uv_done), ("G", wfull)],
                            ev, rider=rider)
                proj(wq_sb, 0, lambda mo, base=base: q_sb[:, mo, base:base + SLT],
                     [("V", fm_ready), ("G", wfull)])
                stA["fm_free_T"] = sT.n

        stB = {"wo_done": 0, "zdma": 0, "zevT": 0}

        def emit_phase_B(seq):
            psTf = psT.bitcast(F32).reshape([128, 1024])
            stb = [ps1[:, 0:1024], psTf[:, 0:1024]]
            av2 = ps2[:, 0:1024]
            drb = [ps2[:, 1024:1536], ps2[:, 1536:2048]]
            ptfl = pt_sb.reshape([128, 2048])

            def pwide(s):
                if s < 2:
                    return ptfl[:, s * 1024:(s + 1) * 1024]
                return xnt_flat[:, (s - 1) * 1024:s * 1024]

            def pslot(s, h):
                if s < 2:
                    return pt_sb[:, s, h, :]
                o = (s - 1) * 1024 + h * SLT
                return xnt_flat[:, o:o + SLT]
            exp_done = {}
            av_done = {}
            attn_done = {}

            def emit_wo(mo, wst):
                pa = ps2[:, 1024 + (mo % 2) * 512:1024 + (mo % 2 + 1) * 512]
                def fz(t, mo=mo, pa=pa, aa=wst["aa"], wv=wst["zev"].get(mo - 2),
                       wwo=wst["wwo"]):
                    t.wait_ge(sV.h, aa)
                    t.wait_ge(qG.h, wwo)
                    if wv is not None:
                        t.wait_ge(sV.h, wv)
                    for ki in range(NCH):
                        r = t.matmul(pa, wo_sb[:, ki, mo * 128:(mo + 1) * 128],
                                     u_s[:, ki, :], start=(ki == 0), stop=(ki == 7))
                    return r
                Q["T"].append(fz)
                tz = inc("T", sT)
                if mo == NCH - 1:
                    stB["wo_done"] = tz
                def fzev(v, mo=mo, pa=pa, tz=tz, wdma=stB["zdma"]):
                    v.wait_ge(sT.h, tz)
                    if wdma:
                        v.wait_ge(qZ.h, wdma)
                    return v.tensor_scalar(z_sb[:, mo, :], pa,
                                           bcols[:, 32 + mo:32 + mo + 1],
                                           None, ALU.add)
                Q["V"].append(fzev)
                wst["zev"][mo] = inc("V", sV)
                if mo == NCH - 1:
                    stB["zevT"] = wst["zev"][mo]
                    zv = wst["zev"][mo]
                    for m2 in range(NCH):
                        qZ.n += 16
                        def fy(y, m2=m2, zv=zv, seq=wst["seq"], tqc=wst["tqc"]):
                            y.wait_ge(sV.h, zv)
                            y.dma_start(out=dzs[seq, m2, :,
                                               tqc * SLT:(tqc + 1) * SLT],
                                        in_=z_sb[:, m2, :]).then_inc(qZ.h, 16)
                        Q["Y"].append(fy)
                    stB["zdma"] = qZ.n

            pend = []
            for tqc in range(4):
                qb = tqc * SLT
                for j in range(8):
                    par = j % 2
                    avA, avB = av2[:, 0:512], av2[:, 512:1024]

                    def emit_qk(tkc):
                        pk = tkc % 2
                        pp = tkc % 4
                        wk_ = (j, tkc - 2) if tkc >= 2 else \
                            ((j - 1, 14 + tkc) if j > 0 else (7, 14 + tkc))
                        we = exp_done.get(wk_)
                        def fqk(t, j=j, tkc=tkc, qb=qb, pk=pk, we=we):
                            if we is not None:
                                t.wait_ge(sS.h, we)
                            t.matmul(stb[pk][:, 0:512],
                                     k_sb[0:64, j, tkc * 128:(tkc + 1) * 128],
                                     q_sb[0:64, j, qb:qb + SLT], start=True, stop=True,
                                     tile_position=(0, 0))
                            return t.matmul(stb[pk][:, 512:1024],
                                            k_sb[64:128, j, tkc * 128:(tkc + 1) * 128],
                                            q_sb[64:128, j, qb:qb + SLT],
                                            start=True, stop=True,
                                            tile_position=(64, 0))
                        Q["T"].append(fqk)
                        qk_done = inc("T", sT)
                        wav = av_done.get((j, tkc - 4)) if tkc >= 4 else \
                            av_done.get((j - 1, 12 + tkc) if j > 0 else (7, 12 + tkc))
                        def fexp(sc, tkc=tkc, pk=pk, pp=pp, qk=qk_done, wav=wav):
                            sc.wait_ge(sT.h, qk)
                            if wav is not None:
                                sc.wait_ge(sT.h, wav)
                            return sc.activation(pwide(pp), stb[pk][:, 0:1024],
                                                 AF.Exp, bias=cst[:, 1:2],
                                                 scale=cst[:, 0:1])
                        Q["S"].append(fexp)
                        exp_done[(j, tkc)] = inc("S", sS)

                    def emit_av(tkc):
                        pp = tkc % 4
                        ed = exp_done[(j, tkc)]
                        wat = attn_done.get(j - 1 if j >= 1 else 7)
                        def fav(t, j=j, tkc=tkc, pp=pp, ed=ed,
                                avA=avA, avB=avB, wat=wat):
                            t.wait_ge(sS.h, ed)
                            if tkc == 0 and wat is not None:
                                t.wait_ge(sV.h, wat)
                            t.matmul(avA[0:65, :], vt_sb[:, tkc, j, 0:65],
                                     pslot(pp, 0),
                                     start=(tkc == 0), stop=(tkc == 15))
                            return t.matmul(avB[0:65, :], vt_sb[:, tkc, j, 65:130],
                                            pslot(pp, 1),
                                            start=(tkc == 0), stop=(tkc == 15))
                        Q["T"].append(fav)
                        av_done[(j, tkc)] = inc("T", sT)

                    AVLAG = 3
                    for tkc in range(16 + AVLAG):
                        if tkc >= AVLAG:
                            emit_av(tkc - AVLAG)
                        if tkc < 16:
                            emit_qk(tkc)
                        if j == 0 and pend and tkc % 2 == 1:
                            emit_wo(*pend.pop(0))
                    while j == 0 and pend:
                        emit_wo(*pend.pop(0))

                    def fden(v, av2=av2, ad=av_done[(j, 15)]):
                        v.wait_ge(sT.h, ad)
                        return v.reciprocal(rden[0:1, :], av2[64:65, :])
                    Q["V"].append(fden)
                    den_done = inc("V", sV)
                    drp = drb[par]
                    def fdr(t, dd=den_done, drp=drp):
                        t.wait_ge(sV.h, dd)
                        t.matmul(drp[0:64, :], ones_r[0:1, 0:64], rden[0:1, 0:512],
                                 start=True, stop=True, tile_position=(0, 0))
                        return t.matmul(drp[64:128, :], ones_r[0:1, 0:64],
                                        rden[0:1, 512:1024],
                                        start=True, stop=True, tile_position=(0, 64))
                    Q["T"].append(fdr)
                    dr_done = inc("T", sT)
                    def fnorm(v, j=j, drd=dr_done, avA=avA, avB=avB, drp=drp,
                              wz=stB["wo_done"]):
                        v.wait_ge(sT.h, drd)
                        if wz:
                            t = None
                            v.wait_ge(sT.h, wz)
                        v.tensor_copy(dr_sb[:], drp)
                        v.drain()
                        v.scalar_tensor_tensor(u_s[0:64, j, :], avA[0:64, :], 1.0,
                                               dr_sb[0:64, :], ALU.mult, ALU.mult)
                        return v.scalar_tensor_tensor(u_s[64:128, j, :], avB[0:64, :],
                                                      1.0, dr_sb[64:128, :],
                                                      ALU.mult, ALU.mult)
                    Q["V"].append(fnorm)
                    attn_done[j] = inc("V", sV)

                wst = {"aa": attn_done[7], "zev": {}, "wwo": qG.n,
                       "seq": seq, "tqc": tqc}
                if tqc < 3:
                    pend = [(mo, wst) for mo in range(NCH)]
                else:
                    for mo in range(NCH):
                        emit_wo(mo, wst)

        def emit_phase_C(cw):
            ztp = [ps1[:, 0:1024], ps2[:, 1024:2048]]
            ztp_keys = [("zA",), ("zB",)]
            tp_bf = [psT[:, 0:1024], psT[:, 1024:2048]]
            ha_ps = ps2[:, 0:512]
            b7 = ps2[:, 512:1024]
            y_banks = [ps2[:, 0:512], ps2[:, 512:1024]]
            stC = {"zld_free": 0, "xt_free": 0, "x2r_free": 0, "ha_free": 0,
                   "b7_free": 0, "ysb_freeT": 0, "out_dma_all": 0}

            for s in range(NSEQ * NSL):
                seq, sl = divmod(s, NSL)
                base = sl * SLT
                gbase = seq * T + base
                for mo in range(NCH):
                    qZ.n += 16
                    def fy(y, mo=mo, seq=seq, base=base, zf=stC["zld_free"],
                           zb=stB["zdma"], xf=stC["x2r_free"]):
                        if zf:
                            y.wait_ge(sT.h, zf)
                        if zb:
                            y.wait_ge(qZ.h, zb)
                        if xf:
                            y.wait_ge(sV.h, xf)
                        y.dma_start(out=z_ld[:, mo, :],
                                    in_=dzs[seq, mo, :, base:base + SLT]
                                    ).then_inc(qZ.h, 16)
                    Q["Y"].append(fy)
                zall = qZ.n
                ztok_ev = {}
                for tt in range(NTT):
                    pk = tt % 2
                    Q["T"].append(lambda t, w=zall: t.wait_ge(qZ.h, w))
                    wait_bank("T", ztp_keys[pk][0])
                    def ftr(t, tt=tt, pk=pk):
                        for c in range(NCH):
                            r = t.transpose(ztp[pk][:, c * 128:(c + 1) * 128],
                                            z_ld[:, c, tt * 128:(tt + 1) * 128],
                                            identf[:])
                        return r
                    Q["T"].append(ftr)
                    ttr = inc("T", sT)
                    def fev(v, tt=tt, pk=pk, w=ttr):
                        v.wait_ge(sT.h, w)
                        v.drain()
                        for hh in range(2):
                            v.tensor_scalar(z_tok[:, tt, hh * 512:(hh + 1) * 512],
                                            ztp[pk][:, hh * 512:(hh + 1) * 512],
                                            1.0, 0.0, ALU.mult, ALU.add,
                                            accum_out=zsp[:, tt, hh:hh + 1])
                            r = v.drain()
                        return r
                    Q["V"].append(fev)
                    vev = inc("V", sV)
                    free_bank(ztp_keys[pk][0], "V", vev)
                    ztok_ev[tt] = vev
                stC["zld_free"] = sT.n
                def fst2(v):
                    v.drain()
                    v.reduce_sum(st1[:], zsp[:], axis=AX.X)
                    v.drain()
                    for tt in range(NTT):
                        v.scalar_tensor_tensor(dummy[:, 0:D], z_tok[:, tt, :], 1.0,
                                               z_tok[:, tt, :], ALU.mult, ALU.mult,
                                               accum_out=st2[:, tt:tt + 1])
                        v.drain()
                    v.tensor_scalar(stm[:], st1[:], 1.0 / D, None, ALU.mult)
                    v.tensor_scalar(stv[:], st2[:], 1.0 / D, None, ALU.mult)
                    v.drain()
                    v.tensor_tensor(stq[:], stm[:], stm[:], ALU.mult)
                    v.drain()
                    v.tensor_sub(stv[:], stv[:], stq[:])
                    v.drain()
                    v.tensor_scalar(stv[:], stv[:], EPS, None, ALU.add)
                    v.drain()
                    return v.reciprocal(stv[:], stv[:])
                Q["V"].append(fst2)
                v2 = inc("V", sV)
                def fsq2(s, v=v2):
                    s.wait_ge(sV.h, v)
                    return s.activation(sti[:], stv[:], AF.Sqrt)
                Q["S"].append(fsq2)
                sd2 = inc("S", sS)
                def fnm2(v, w=sd2):
                    v.wait_ge(sS.h, w)
                    v.drain()
                    v.scalar_tensor_tensor(stn[:], stm[:], -1.0, sti[:],
                                           ALU.mult, ALU.mult)
                    return v.drain()
                Q["V"].append(fnm2)
                sd2_n = inc("V", sV)
                for tt in range(NTT):
                    qX.n += 16
                    def fy(y, tt=tt, seq=seq, base=base, xf=stC["x2r_free"]):
                        if xf:
                            y.wait_ge(sV.h, xf)
                        y.dma_start(out=xt[:, tt, :],
                                    in_=dx[seq, base + tt * 128:base + (tt + 1) * 128, :]
                                    ).then_inc(qX.h, 16)
                    Q["Y"].append(fy)
                xload_all = qX.n
                x2dma = []
                for tt in range(NTT):
                    def f2a(sc, tt=tt, wn=sd2_n):
                        sc.wait_ge(sV.h, wn)
                        return sc.activation(xn2_t4[:, tt, :], z_tok[:, tt, :],
                                             AF.Identity, bias=stn[:, tt:tt + 1],
                                             scale=sti[:, tt:tt + 1])
                    Q["S"].append(f2a)
                    sx2 = inc("S", sS)
                    def f2(v, tt=tt, sx2=sx2, xl=xload_all):
                        v.wait_ge(sS.h, sx2)
                        v.wait_ge(qX.h, xl)
                        v.drain()
                        v.scalar_tensor_tensor(x2_sl[:, tt, :], xt[:, tt, :], 1.0,
                                               xn2_t4[:, tt, :], ALU.mult, ALU.add,
                                               accum_out=st1[:, tt:tt + 1])
                        v.drain()
                        v.scalar_tensor_tensor(dummy[:, 0:D], x2_sl[:, tt, :],
                                               1.0, x2_sl[:, tt, :],
                                               ALU.mult, ALU.mult,
                                               accum_out=st2[:, tt:tt + 1])
                        return v.drain()
                    Q["V"].append(f2)
                    vx2 = inc("V", sV)
                    if tt == NTT - 1:
                        stC["xt_free"] = vx2
                    qW.n += 16
                    def fy2(y, tt=tt, vx2=vx2, gbase=gbase):
                        y.wait_ge(sV.h, vx2)
                        y.dma_start(out=dx2[gbase + tt * 128:gbase + (tt + 1) * 128, :],
                                    in_=x2_sl[:, tt, :]).then_inc(qW.h, 16)
                    Q["Y"].append(fy2)
                    x2dma.append(qW.n)
                def fst3(v):
                    v.tensor_scalar(stm[:], st1[:], 1.0 / D, None, ALU.mult)
                    v.tensor_scalar(stv[:], st2[:], 1.0 / D, None, ALU.mult)
                    v.drain()
                    v.tensor_tensor(stq[:], stm[:], stm[:], ALU.mult)
                    v.drain()
                    v.tensor_sub(stv[:], stv[:], stq[:])
                    v.drain()
                    v.tensor_scalar(stv[:], stv[:], EPS, None, ALU.add)
                    v.drain()
                    return v.reciprocal(stv[:], stv[:])
                Q["V"].append(fst3)
                v3 = inc("V", sV)
                def fsq3(s, v=v3):
                    s.wait_ge(sV.h, v)
                    return s.activation(sti[:], stv[:], AF.Sqrt)
                Q["S"].append(fsq3)
                sd3 = inc("S", sS)
                def fnm3(v, w=sd3):
                    v.wait_ge(sS.h, w)
                    v.drain()
                    v.scalar_tensor_tensor(stn[:], stm[:], -1.0, sti[:],
                                           ALU.mult, ALU.mult)
                    return v.drain()
                Q["V"].append(fnm3)
                sd3_n = inc("V", sV)
                for tt in range(NTT):
                    def fa3(v, tt=tt, sd3=sd3):
                        v.wait_ge(sS.h, sd3)
                        v.drain()
                        return v.tensor_scalar(xn3_tok[:, tt % 2, :], x2_sl[:, tt, :],
                                               stm[:, tt:tt + 1], sti[:, tt:tt + 1],
                                               ALU.subtract, ALU.mult)
                    Q["V"].append(fa3)
                    va3 = inc("V", sV)
                    bank = "p%d" % (tt % 2)
                    tpb = tp_bf[tt % 2]
                    Q["T"].append(lambda t, v=va3: t.wait_ge(sV.h, v))
                    wait_bank("T", bank)
                    def ftr3(t, tpb=tpb, tt=tt):
                        for c in range(NCH):
                            r = t.transpose(tpb[:, c * 128:(c + 1) * 128],
                                            xn3_tok[:, tt % 2, c * 128:(c + 1) * 128],
                                            identb[:])
                        return r
                    Q["T"].append(ftr3)
                    ttr3 = inc("T", sT)
                    def fev3(v, tpb=tpb, tt=tt, w=ttr3, wf=stC.get("fm_free_T", 0)):
                        v.wait_ge(sT.h, w)
                        if wf:
                            v.wait_ge(sT.h, wf)
                        return v.tensor_copy(xn_fm[:, :, tt * 128:(tt + 1) * 128],
                                             tpb.rearrange("p (c t) -> p c t", c=NCH))
                    Q["V"].append(fev3)
                    vev3 = inc("V", sV)
                    free_bank(bank, "V", vev3)
                fm3_all = sV.n

                def fha(t, fm=fm3_all, wld=cw["astk"], hf=stC["ha_free"]):
                    t.wait_ge(sV.h, fm)
                    t.wait_ge(qG.h, wld)
                    if hf:
                        t.wait_ge(sV.h, hf)
                    for ki in range(NCH):
                        r = t.matmul(ha_ps, astk_a[:, ki, 0:128], xn_fm[:, ki, :],
                                     start=(ki == 0), stop=(ki == 7))
                    return r
                Q["T"].append(fha)
                tha = inc("T", sT)
                def fhl(t, hf=stC["b7_free"]):
                    if hf:
                        t.wait_ge(sV.h, hf)
                    for ki in range(NCH):
                        r = t.matmul(b7[0:6, 0:512], astk_a[:, ki, 128:134],
                                     xn_fm[:, ki, :], start=(ki == 0), stop=(ki == 7))
                    return r
                Q["T"].append(fhl)
                thl = inc("T", sT)
                def flg(v, w=thl):
                    v.wait_ge(sT.h, w)
                    return v.tensor_scalar(lg_sb, b7[0:6, 0:512], rbcol[0:6, :],
                                           None, ALU.add)
                Q["V"].append(flg)
                vlg = inc("V", sV)
                def fltr(t, w=vlg):
                    t.wait_ge(sV.h, w)
                    for tt in range(NTT):
                        r = t.transpose(b7[:, tt * 6:(tt + 1) * 6],
                                        lg_sb[:, tt * 128:(tt + 1) * 128],
                                        identf[0:6, 0:6])
                    return r
                Q["T"].append(fltr)
                tltr = inc("T", sT)
                def flgt(v, w=tltr):
                    v.wait_ge(sT.h, w)
                    return v.tensor_copy(lg_tok[:].rearrange("p t e -> p (t e)"),
                                         b7[:, 0:24])
                Q["V"].append(flgt)
                vlgt = inc("V", sV)
                def fge(sc, w=vlgt):
                    sc.wait_ge(sV.h, w)
                    return sc.activation(ge_sb[:].rearrange("p t e -> p (t e)"),
                                         lg_tok[:].rearrange("p t e -> p (t e)"), AF.Exp)
                Q["S"].append(fge)
                sge = inc("S", sS)
                def fgm(v, w=sge):
                    v.wait_ge(sS.h, w)
                    v.drain()
                    v.reduce_max(g_m1[:], ge_sb[:], axis=AX.X)
                    v.drain()
                    for e in range(6):
                        v.tensor_tensor(g_t1[:, :, e], ge_sb[:, :, e], g_m1[:],
                                        ALU.is_equal)
                        v.drain()
                    v.tensor_mul(g_t2[:], ge_sb[:], g_t1[:])
                    v.drain()
                    v.tensor_sub(g_t2[:], ge_sb[:], g_t2[:])
                    v.drain()
                    v.reduce_max(g_m2[:], g_t2[:], axis=AX.X)
                    v.drain()
                    v.reduce_sum(g_s[:], ge_sb[:], axis=AX.X)
                    v.drain()
                    v.tensor_add(g_rd[:], g_m1[:], g_m2[:])
                    v.drain()
                    v.scalar_tensor_tensor(g_rd[:], g_s[:], 1e-6, g_rd[:],
                                           ALU.mult, ALU.add)
                    v.drain()
                    v.reciprocal(g_rd[:], g_rd[:])
                    v.drain()
                    for e in range(6):
                        v.tensor_tensor(g_t1[:, :, e], ge_sb[:, :, e], g_m2[:],
                                        ALU.is_ge)
                        v.drain()
                    v.tensor_mul(g_t2[:], ge_sb[:], g_t1[:])
                    v.drain()
                    for e in range(6):
                        r = v.tensor_tensor(w_tok[:, :, e], g_t2[:, :, e], g_rd[:],
                                            ALU.mult)
                    return r
                Q["V"].append(fgm)
                vw = inc("V", sV)
                def fwtr(t, w=vw):
                    t.wait_ge(sV.h, w)
                    for tt in range(NTT):
                        r = t.transpose(b7[0:6, tt * 128:(tt + 1) * 128],
                                        w_tok[:, tt, :], identf[:])
                    return r
                Q["T"].append(fwtr)
                twtr = inc("T", sT)
                def fwfm(v, w=twtr):
                    v.wait_ge(sT.h, w)
                    return v.tensor_copy(w_fm, b7[0:6, 0:512])
                Q["V"].append(fwfm)
                vwfm = inc("V", sV)
                def fwrep(t, w=vwfm):
                    t.wait_ge(sV.h, w)
                    return t.matmul(b7, e16[:], w_fm, start=True, stop=True)
                Q["T"].append(fwrep)
                twrep = inc("T", sT)
                def fbr(v, w1=tha, w2=twrep):
                    v.wait_ge(sT.h, w1)
                    v.wait_ge(sT.h, w2)
                    v.tensor_copy(hs_brhs, ha_ps)
                    v.drain()
                    v.tensor_tensor(hs_brhs[32:64, :], hs_brhs[32:64, :],
                                    b7[32:64, :], ALU.mult)
                    return v.tensor_tensor(hs_brhs[64:128, :], hs_brhs[64:128, :],
                                            b7[64:128, :], ALU.mult)
                Q["V"].append(fbr)
                vbr = inc("V", sV)
                stC["ha_free"] = vbr
                stC["b7_free"] = vbr
                yev = {}
                for mo in range(NCH):
                    pa = y_banks[mo % 2]
                    def fy(t, mo=mo, pa=pa, vbr=vbr, fm=fm3_all,
                           wv=yev.get(mo - 2), wl1=cw["bwT"], wl2=cw["bstk"],
                           wts=stC["ysb_freeT"]):
                        t.wait_ge(sV.h, vbr)
                        t.wait_ge(sV.h, fm)
                        t.wait_ge(qG.h, max(wl1, wl2))
                        if wv is not None:
                            t.wait_ge(sV.h, wv)
                        for ki in range(NCH):
                            t.matmul(pa, bw_sb[:, ki, mo * 128:(mo + 1) * 128],
                                     xn_fm[:, ki, :], start=(ki == 0), stop=False)
                        return t.matmul(pa, bstk_a[:, mo, 134:262], hs_brhs,
                                        start=False, stop=True)
                    Q["T"].append(fy)
                    ty = inc("T", sT)
                    def fyev(v, mo=mo, pa=pa, w=ty, wts=stC["ysb_freeT"]):
                        v.wait_ge(sT.h, w)
                        if wts:
                            v.wait_ge(sT.h, wts)
                        return v.tensor_scalar(y_sb[:, mo, :], pa,
                                               bcols[:, 40 + mo:40 + mo + 1],
                                               None, ALU.add)
                    Q["V"].append(fyev)
                    yev[mo] = inc("V", sV)
                y_all = yev[NCH - 1]
                for tt in range(NTT):
                    pk = tt % 2
                    Q["T"].append(lambda t, w=y_all: t.wait_ge(sV.h, w))
                    wait_bank("T", ztp_keys[pk][0])
                    def ftr4(t, tt=tt, pk=pk):
                        for c in range(NCH):
                            r = t.transpose(ztp[pk][:, c * 128:(c + 1) * 128],
                                            y_sb[:, c, tt * 128:(tt + 1) * 128],
                                            identf[:])
                        return r
                    Q["T"].append(ftr4)
                    ttr4 = inc("T", sT)
                    def fev4(v, tt=tt, pk=pk, w=ttr4):
                        v.wait_ge(sT.h, w)
                        v.drain()
                        for hh in range(2):
                            v.tensor_scalar(y_tok[:, tt, hh * 512:(hh + 1) * 512],
                                            ztp[pk][:, hh * 512:(hh + 1) * 512],
                                            1.0, 0.0, ALU.mult, ALU.add,
                                            accum_out=zsp[:, tt, hh:hh + 1])
                            r = v.drain()
                        return r
                    Q["V"].append(fev4)
                    vev4 = inc("V", sV)
                    free_bank(ztp_keys[pk][0], "V", vev4)
                stC["ysb_freeT"] = sT.n
                stC["fm_free_T"] = sT.n
                def fst4(v):
                    v.drain()
                    v.reduce_sum(st1[:], zsp[:], axis=AX.X)
                    v.drain()
                    for tt in range(NTT):
                        v.scalar_tensor_tensor(dummy[:, 0:D], y_tok[:, tt, :], 1.0,
                                               y_tok[:, tt, :], ALU.mult, ALU.mult,
                                               accum_out=st2[:, tt:tt + 1])
                        v.drain()
                    v.tensor_scalar(stm[:], st1[:], 1.0 / D, None, ALU.mult)
                    v.tensor_scalar(stv[:], st2[:], 1.0 / D, None, ALU.mult)
                    v.drain()
                    v.tensor_tensor(stq[:], stm[:], stm[:], ALU.mult)
                    v.drain()
                    v.tensor_sub(stv[:], stv[:], stq[:])
                    v.drain()
                    v.tensor_scalar(stv[:], stv[:], EPS, None, ALU.add)
                    v.drain()
                    return v.reciprocal(stv[:], stv[:])
                Q["V"].append(fst4)
                v4 = inc("V", sV)
                def fsq4(s, v=v4):
                    s.wait_ge(sV.h, v)
                    return s.activation(sti[:], stv[:], AF.Sqrt)
                Q["S"].append(fsq4)
                sd4 = inc("S", sS)
                def fnm4(v, w=sd4):
                    v.wait_ge(sS.h, w)
                    v.drain()
                    v.scalar_tensor_tensor(stn[:], stm[:], -1.0, sti[:],
                                           ALU.mult, ALU.mult)
                    return v.drain()
                Q["V"].append(fnm4)
                sd4_n = inc("V", sV)
                for tt in range(NTT):
                    qX.n += 16
                    def fyx(y, tt=tt, gbase=gbase, ww=x2dma[-1], xf=stC["xt_free"]):
                        y.wait_ge(qW.h, ww)
                        if xf:
                            y.wait_ge(sV.h, xf)
                        y.dma_start(out=x2r[:, tt, :],
                                    in_=dx2[gbase + tt * 128:gbase + (tt + 1) * 128, :]
                                    ).then_inc(qX.h, 16)
                    Q["Y"].append(fyx)
                x2l_all = qX.n
                for tt in range(NTT):
                    def ffin(v, tt=tt, sd4=sd4, xl=x2l_all,
                             wod=stC["out_dma_all"]):
                        v.wait_ge(sS.h, sd4)
                        v.wait_ge(qX.h, xl)
                        if wod:
                            v.wait_ge(qO.h, wod)
                        v.drain()
                        v.tensor_scalar(xn2_t4[:, tt, :], y_tok[:, tt, :],
                                        stm[:, tt:tt + 1], sti[:, tt:tt + 1],
                                        ALU.subtract, ALU.mult)
                        v.drain()
                        return v.tensor_add(out_sb[:, tt, :], x2r[:, tt, :],
                                            xn2_t4[:, tt, :])
                    Q["V"].append(ffin)
                    vfin = inc("V", sV)
                    if tt == NTT - 1:
                        stC["x2r_free"] = vfin
                    qO.n += 16
                    def fyo(y, tt=tt, vfin=vfin, seq=seq, base=base):
                        y.wait_ge(sV.h, vfin)
                        y.dma_start(out=dout[seq, base + tt * 128:base + (tt + 1) * 128, :],
                                    in_=out_sb[:, tt, :]).then_inc(qO.h, 16)
                    Q["Y"].append(fyo)
                stC["out_dma_all"] = qO.n

        phase_list = {"A": "A0", "AB": "A0 B0", "ABA": "A0 B0 A1",
                      "ABAB": "A0 B0 A1 B1",
                      "FULL": "A0 B0 A1 B1 C"}[KPHASES].split()
        for ph in phase_list:
            if ph[0] == "A":
                seq = int(ph[1])
                if seq > 0:
                    load_w("wq", wq_sb)
                emit_phase_A(seq)
                barrier()
                if KDBG and seq == 0:
                    for dst, src in [(ddbg_q, q_sb[:]), (ddbg_k, k_sb[:]),
                                     (ddbg_v, vt_sb[:]), (ddbg_w, wq_sb),
                                     (ddbg_x, xn_fm[:]), (ddbg_t, xn_tok[:]),
                                     (ddbg_s, None)]:
                        if src is None:
                            for i2, tsr in enumerate((st1, st2, stm, stv, sti)):
                                qO.n += 16
                                Q["Y"].append(lambda y, d=ddbg_s[:, i2, :], s2=tsr:
                                              y.dma_start(out=d, in_=s2[:]).then_inc(qO.h, 16))
                            continue
                        qO.n += 16
                        Q["Y"].append(lambda y, d=dst, s=src:
                                      y.dma_start(out=d, in_=s).then_inc(qO.h, 16))
                    barrier()
            elif ph[0] == "B":
                seq = int(ph[1])
                load_w("wo", wo_sb)
                barrier()
                emit_phase_B(seq)
                barrier()
                if KDBG and seq == 0:
                    qO.n += 16
                    Q["Y"].append(lambda y: y.dma_start(out=ddbg_a, in_=u_s[:]
                                                        ).then_inc(qO.h, 16))
                    qO.n += 16
                    Q["Y"].append(lambda y: y.dma_start(out=ddbg_z0, in_=dzs[0]
                                                        ).then_inc(qO.h, 16))
                    barrier()

        cw = {}
        do_c = "C" in phase_list
        for name, dstap, srcap in ([] if not do_c else [
            ("bwT", bw_sb, dw["bwT"].ap().rearrange("(k p) o -> p k o", p=128)),
            ("astk", astk_a[:, :, 0:134],
             dastk.ap().rearrange("(k p) o -> p k o", p=128)),
            ("bstk", bstk_a[:, :, 134:262],
             dbstk.ap().rearrange("p (m c) -> p m c", m=NCH)),
        ]):
            qG.n += 16
            Q["G"].append(lambda g, d=dstap, s=srcap: g.dma_start(out=d, in_=s).then_inc(qG.h, 16))
            cw[name] = qG.n
        if do_c:
            barrier()
            emit_phase_C(cw)

        if KDBG and "C" in phase_list:
            barrier()
            for d_, s_ in [(ddbg_l, lg_tok[:]), (ddbg_g, w_tok[:]), (ddbg_e, ge_sb[:])]:
                qO.n += 16
                Q["Y"].append(lambda y, d=d_, s2=s_: y.dma_start(out=d, in_=s2).then_inc(qO.h, 16))
        totals = [(qC, qC.n), (qX, qX.n), (qZ, qZ.n), (qW, qW.n),
                  (qO, qO.n), (qG, qG.n)]
        def ffinal(g, totals=tuple((t[0].h, t[1]) for t in totals)):
            for h, v in totals:
                if v:
                    g.wait_ge(h, v)
        Q["G"].append(ffinal)

        block = ctx.enter_context(nc.Block())

        @block.tensor
        def _(t):
            for f in Q["T"]:
                f(t)

        @block.vector
        def _(v):
            for f in Q["V"]:
                f(v)

        @block.scalar
        def _(s):
            for f in Q["S"]:
                f(s)

        @block.gpsimd
        def _(g):
            for f in Q["G"]:
                f(g)

        @block.sync
        def _(y):
            for f in Q["Y"]:
                f(y)

    return nc


def _prep(inp):
    f32 = lambda a: np.ascontiguousarray(np.asarray(a), dtype=np.float32)
    x = f32(inp["L0_attn_x"])
    Wq, bq = f32(inp["L0_attn_Wq"]), f32(inp["L0_attn_bq"])
    Wk, bk = f32(inp["L0_attn_Wk"]), f32(inp["L0_attn_bk"])
    Wv, bv = f32(inp["L0_attn_Wv"]), f32(inp["L0_attn_bv"])
    Wl, bl = f32(inp["L0_attn_Wl"]), f32(inp["L0_attn_bl"])
    Wo, bo = f32(inp["L0_attn_Wo"]), f32(inp["L0_attn_bo"])
    bW, bb = f32(inp["L0_ffn_base_W"]), f32(inp["L0_ffn_base_b"])
    shA, shB = f32(inp["L0_ffn_shared_A"]), f32(inp["L0_ffn_shared_B"])
    rA, rB = f32(inp["L0_ffn_routed_A"]), f32(inp["L0_ffn_routed_B"])
    rW, rb = f32(inp["L0_ffn_router_W"]), f32(inp["L0_ffn_router_b"])

    astk = np.zeros((D, 134), np.float32)
    astk[:, 0:16] = shA.T
    for e in range(6):
        astk[:, 32 + 16 * e:48 + 16 * e] = rA[e].T
    astk[:, 128:134] = rW.T
    bstk = np.zeros((128, D), np.float32)
    bstk[0:16] = shB.T
    for e in range(6):
        bstk[32 + 16 * e:48 + 16 * e] = rB[e].T
    bcols = np.zeros((128, 48), np.float32)
    for i, b in enumerate([bq, bk, bv, bl, bo, bb]):
        bcols[:, 8 * i:8 * (i + 1)] = b.reshape(8, 128).T
    rbcol = np.zeros((128, 1), np.float32)
    rbcol[0:6, 0] = rb
    e16aug = np.zeros((6, 128), np.float32)
    for e in range(6):
        e16aug[e, 32 + 16 * e:48 + 16 * e] = 1.0
    consts = np.zeros((128, 4), np.float32)
    consts[:, 0] = 0.125
    consts[:, 1] = -SHIFT
    common = dict(
        wq=Wq, wk=Wk, wv=Wv, wl=Wl, wo=Wo, bwT=np.ascontiguousarray(bW.T),
        astk=astk, bstk=bstk, bcols=bcols, rbcol=rbcol,
        blrow=bl.reshape(1, D), onesr=np.ones((1, 128), np.float32),
        identbf=np.eye(128, dtype=ml_dtypes.bfloat16),
        identf=np.eye(128, dtype=np.float32),
        e16aug=e16aug, consts=consts,
    )
    return x, common


def kernel(**inp):
    x, common = _prep(inp)
    nc = build_nc()
    in_maps = [dict(common, x=np.ascontiguousarray(x[2 * c:2 * c + 2]))
               for c in range(8)]
    res = run_bass_kernel_spmd(nc, in_maps, core_ids=list(range(8)))
    out = np.concatenate([r["out"] for r in res.results], axis=0)
    return out.reshape(16, T, D)


# version 12
# speedup vs baseline: 1.0440x; 1.0051x over previous
import numpy as np
import ml_dtypes
from contextlib import ExitStack

import os
import concourse.bass as bass
import concourse.mybir as mybir
from concourse.bass_utils import run_bass_kernel_spmd

KPHASES = os.environ.get("KPHASES", "FULL")
KDBG = bool(int(os.environ.get("KDBG", "0")))

F32 = mybir.dt.float32
BF16 = mybir.dt.bfloat16
AF = mybir.ActivationFunctionType
ALU = mybir.AluOpType
AX = mybir.AxisListType

NSEQ = 2
T = 2048
D = 1024
NCH = 8
NSL = 4
SLT = 512
NTT = 4
EPS = 1e-5
SHIFT = 3.0

ENGINES = ("T", "V", "S", "G", "Y")


def build_nc():
    nc = bass.Bass("TRN2", target_bir_lowering=False, detect_race_conditions=False)

    dx = nc.dram_tensor("x", [NSEQ, T, D], F32, kind="ExternalInput").ap()
    dw = {n: nc.dram_tensor(n, [D, D], F32, kind="ExternalInput")
          for n in ("wq", "wk", "wv", "wl", "wo", "bwT")}
    dastk = nc.dram_tensor("astk", [D, 134], F32, kind="ExternalInput")
    dbstk = nc.dram_tensor("bstk", [128, D], F32, kind="ExternalInput")
    dbcols = nc.dram_tensor("bcols", [128, 48], F32, kind="ExternalInput").ap()
    drbcol = nc.dram_tensor("rbcol", [128, 1], F32, kind="ExternalInput").ap()
    dblrow = nc.dram_tensor("blrow", [1, D], F32, kind="ExternalInput").ap()
    dones = nc.dram_tensor("onesr", [1, 128], F32, kind="ExternalInput").ap()
    didb = nc.dram_tensor("identbf", [128, 128], BF16, kind="ExternalInput").ap()
    didf = nc.dram_tensor("identf", [128, 128], F32, kind="ExternalInput").ap()
    de16 = nc.dram_tensor("e16aug", [6, 128], F32, kind="ExternalInput").ap()
    dcst = nc.dram_tensor("consts", [128, 4], F32, kind="ExternalInput").ap()
    scratch_kind = "ExternalOutput" if KDBG else "Internal"
    dzs = nc.dram_tensor("z_scr", [NSEQ, NCH, 128, T], F32, kind=scratch_kind).ap()
    dx2 = nc.dram_tensor("x2_scr", [NSEQ * T, D], F32, kind=scratch_kind).ap()
    dout = nc.dram_tensor("out", [NSEQ, T, D], F32, kind="ExternalOutput").ap()
    if KDBG:
        ddbg_q = nc.dram_tensor("dbg_q", [128, NCH, T], BF16, kind="ExternalOutput").ap()
        ddbg_k = nc.dram_tensor("dbg_k", [128, NCH, T], BF16, kind="ExternalOutput").ap()
        ddbg_v = nc.dram_tensor("dbg_v", [128, 16, 8, 130], BF16, kind="ExternalOutput").ap()
        ddbg_a = nc.dram_tensor("dbg_a", [128, NCH, SLT], BF16, kind="ExternalOutput").ap()
        ddbg_w = nc.dram_tensor("dbg_w", [128, NCH, D], BF16, kind="ExternalOutput").ap()
        ddbg_x = nc.dram_tensor("dbg_x", [128, NCH, SLT], BF16, kind="ExternalOutput").ap()
        ddbg_t = nc.dram_tensor("dbg_t", [128, 2, D], BF16, kind="ExternalOutput").ap()
        ddbg_s = nc.dram_tensor("dbg_s", [128, 5, NTT], F32, kind="ExternalOutput").ap()
        ddbg_z0 = nc.dram_tensor("dbg_z0", [NCH, 128, T], F32, kind="ExternalOutput").ap()
        ddbg_l = nc.dram_tensor("dbg_l", [128, NTT, 6], F32, kind="ExternalOutput").ap()
        ddbg_g = nc.dram_tensor("dbg_g", [128, NTT, 6], F32, kind="ExternalOutput").ap()
        ddbg_e = nc.dram_tensor("dbg_e", [128, NTT, 6], F32, kind="ExternalOutput").ap()

    with ExitStack() as ctx:
        def sb(name, shape, dtype, **kw):
            return ctx.enter_context(nc.sbuf_tensor(name, shape, dtype, **kw))

        arena1 = sb("arena1", [128, 3, NCH, D], BF16)
        wl_sb = sb("wl_sb", [128, NCH, D], BF16)
        wq_sb, wk_sb, wv_sb = arena1[:, 0], arena1[:, 1], arena1[:, 2]
        wo_sb = arena1[:, 0]
        bw_sb = wl_sb[:]
        astk_a = arena1[:, 1]
        bstk_a = arena1[:, 2]

        bcols = sb("bcols_sb", [128, 48], F32)
        rbcol = sb("rbcol_sb", [128, 1], F32)
        ones_r = sb("ones_sb", [1, 128], F32)
        identb = sb("identb_sb", [128, 128], BF16)
        ones_bf = sb("ones_bf", [1, 128], BF16)
        blrow_bf = sb("blrow_bf", [1, D], BF16)
        identf = sb("identf_sb", [128, 128], F32)
        e16 = sb("e16_sb", [6, 128], F32)
        cst = sb("cst_sb", [128, 4], F32)

        xst = sb("xst", [128, NTT, D], F32)
        st1 = sb("st1", [128, NTT], F32)
        st2 = sb("st2", [128, NTT], F32)
        stm = sb("stm", [128, NTT], F32)
        stv = sb("stv", [128, NTT], F32)
        sti = sb("sti", [128, NTT], F32)
        stq = sb("stq", [128, NTT], F32)
        stn = sb("stn", [128, NTT], F32)
        xn_tok = sb("xn_tok", [128, 3, D], BF16)
        xn_fm = sb("xn_fm", [128, NCH, SLT], BF16)
        u_s = sb("u_s", [128, NCH, SLT], BF16)
        q_sb = sb("q_sb", [128, NCH, T], BF16)
        k_sb = sb("k_sb", [128, NCH, T], BF16)
        vt_sb = sb("vt_sb", [128, 16, 8, 130], BF16)
        pt_sb = sb("pt_sb", [128, 2, 2, SLT], BF16)
        pth = pt_sb.bitcast(F32).reshape([128, 1024])
        dummy = pth[:, 0:1024]
        zsp = sb("zsp", [128, NTT, 2], F32)
        lg_tok = sb("lg_tok", [128, NTT, 6], F32)
        ge_sb = sb("ge_sb", [128, NTT, 6], F32)
        g_t1 = sb("g_t1", [128, NTT, 6], F32)
        g_t2 = sb("g_t2", [128, NTT, 6], F32)
        g_m1 = sb("g_m1", [128, NTT], F32)
        g_m2 = sb("g_m2", [128, NTT], F32)
        g_s = sb("g_s", [128, NTT], F32)
        g_rd = sb("g_rd", [128, NTT], F32)
        w_tok = sb("w_tok", [128, NTT, 6], F32)

        z_sb = xst[:].rearrange("p a b -> p (a b)").rearrange(
            "p (m c) -> p m c", m=NCH)
        z_ld = z_sb
        kh = k_sb.bitcast(F32).reshape([128, 8192])
        z_tok = kh[:, 0:4096].rearrange("p (t d) -> p t d", t=NTT)
        y_tok = z_tok
        xt = kh[:, 4096:8192].rearrange("p (t d) -> p t d", t=NTT)
        x2r = xt
        qh = q_sb.bitcast(F32).reshape([128, 8192])
        x2_sl = qh[:, 0:4096].rearrange("p (t d) -> p t d", t=NTT)
        out_sb = qh[:, 4096:8192].rearrange("p (t d) -> p t d", t=NTT)
        vh = vt_sb.bitcast(F32).reshape([128, 8320])
        y_sb = vh[:, 0:4096].rearrange("p (m c) -> p m c", m=NCH)
        uh = u_s.bitcast(F32).reshape([128, 2048])
        xn2_t = uh[:, 0:1024]
        xn2_t4 = xst[:]
        fmh = xn_fm.bitcast(F32).reshape([128, 2048])
        den_sb = fmh[0:1, 0:1024]
        rden = fmh[0:1, 1024:2048]
        xth = xn_tok.bitcast(F32).reshape([128, 1536])
        xnt_flat = xn_tok.reshape([128, 3072])
        dr_sb = xth[:, 0:512]
        ur = u_s.reshape([128, 4096])
        xn3_tok = ur[:, 2048:4096].rearrange("p (t d) -> p t d", t=2)
        hs_brhs = pt_sb.reshape([128, 2048])[:, 0:SLT]
        lg_sb = xth[0:6, 0:512]
        w_fm = xth[0:6, 512:1024]

        ps1 = ctx.enter_context(nc.psum_tensor("ps1", [128, 1024], F32))
        psT = ctx.enter_context(nc.psum_tensor("psT", [128, 2048], BF16))
        ps2 = ctx.enter_context(nc.psum_tensor("ps2", [128, 2048], F32))

        class Sem:
            def __init__(self, name):
                self.h = ctx.enter_context(nc.semaphore(name))
                self.n = 0

        sT, sV, sS = Sem("sT"), Sem("sV"), Sem("sS")
        qG = Sem("qG")
        qC = Sem("qC")
        qX = Sem("qX")
        qZ = Sem("qZ")
        qW = Sem("qW")
        qO = Sem("qO")

        Q = {e: [] for e in ENGINES}

        def inc(e, sem):
            sem.n += 1
            if e in ("V", "S"):
                Q[e].append(lambda eng, sh=sem.h: eng.drain().then_inc(sh, 1))
                return sem.n
            prev = Q[e].pop()
            def wrapped(eng, prev=prev, sh=sem.h):
                r = prev(eng)
                assert r is not None, (
                    "thunk must return its last instruction: %s:%d" %
                    (prev.__code__.co_name, prev.__code__.co_firstlineno))
                r.then_inc(sh, 1)
                return r
            Q[e].append(wrapped)
            return sem.n

        def barrier():
            snap = [(sT, sT.n), (sV, sV.n), (sS, sS.n)]
            names = {id(sT): "T", id(sV): "V", id(sS): "S"}
            dsnap = [(q, q.n) for q in (qC, qX, qZ, qW, qO, qG) if q.n]
            for e in ENGINES:
                for sem, v in snap:
                    if v and names[id(sem)] != e:
                        Q[e].append(lambda eng, sh=sem.h, vv=v: eng.wait_ge(sh, vv))
                for sem, v in dsnap:
                    Q[e].append(lambda eng, sh=sem.h, vv=v: eng.wait_ge(sh, vv))

        bankfree = {}

        def wait_bank(e, key):
            ent = bankfree.get(key)
            if ent is not None:
                sem = {"V": sV, "T": sT, "S": sS}[ent[0]]
                Q[e].append(lambda eng, s=sem.h, v=ent[1]: eng.wait_ge(s, v))

        def free_bank(key, e, val):
            bankfree[key] = (e, val)

        for sbuf, src in [(bcols, dbcols), (rbcol, drbcol),
                          (ones_r, dones), (identb, didb), (identf, didf),
                          (e16, de16), (cst, dcst)]:
            qC.n += 16
            Q["Y"].append(lambda y, d=sbuf, s=src: y.dma_start(out=d[:], in_=s).then_inc(qC.h, 16))
        consts_done = qC.n

        wload = {}

        def load_w(name, dst):
            sap = dw[name].ap().rearrange("(k p) o -> p k o", p=128)
            qG.n += 16
            Q["G"].append(lambda g, d=dst, s=sap: g.dma_start(out=d, in_=s).then_inc(qG.h, 16))
            wload[name] = qG.n

        for name, dst in [("wq", wq_sb), ("wk", wk_sb), ("wv", wv_sb),
                          ("wl", wl_sb[:])]:
            load_w(name, dst)
        qG.n += 16
        Q["G"].append(lambda g: g.dma_start(out=ones_bf[:], in_=dones).then_inc(qG.h, 16))
        qG.n += 16
        Q["G"].append(lambda g: g.dma_start(out=blrow_bf[:], in_=dblrow).then_inc(qG.h, 16))
        wload_bf = qG.n

        stA = {"xst_free": 0, "fm_free_T": 0, "mmfree": {}}

        def emit_phase_A(seq):
            Q["V"].append(lambda v: v.memset(vt_sb[:, :, :, 64:65], 1.0))
            Q["V"].append(lambda v: v.memset(vt_sb[:, :, :, 129:130], 1.0))
            mmfree = stA["mmfree"]

            for sl in range(NSL):
                base = sl * SLT
                for tt in range(NTT):
                    qX.n += 16
                    def fy(y, seq=seq, base=base, tt=tt, xf=stA["xst_free"],
                           zb=stB["zdma"]):
                        y.wait_ge(sV.h, xf)
                        if zb:
                            y.wait_ge(qZ.h, zb)
                        y.dma_start(out=xst[:, tt, :],
                                    in_=dx[seq, base + tt * 128:base + (tt + 1) * 128, :]
                                    ).then_inc(qX.h, 16)
                    Q["Y"].append(fy)
                tload_all = qX.n
                for tt in range(NTT):
                    def f1(v, tt=tt, w=tload_all, cd=consts_done):
                        v.wait_ge(qX.h, w)
                        if tt == 0:
                            v.wait_ge(qC.h, cd)
                        v.drain()
                        v.tensor_scalar(dummy[:, 0:D], xst[:, tt, :], 1.0, 0.0,
                                        ALU.mult, ALU.add,
                                        accum_out=st1[:, tt:tt + 1])
                        v.drain()
                        v.scalar_tensor_tensor(dummy[:, 0:D], xst[:, tt, :], 1.0,
                                               xst[:, tt, :], ALU.mult, ALU.mult,
                                               accum_out=st2[:, tt:tt + 1])
                        v.drain()
                    Q["V"].append(f1)

                def fstat(v):
                    v.tensor_scalar(stm[:], st1[:], 1.0 / D, None, ALU.mult)
                    v.tensor_scalar(stv[:], st2[:], 1.0 / D, None, ALU.mult)
                    v.drain()
                    v.tensor_tensor(stq[:], stm[:], stm[:], ALU.mult)
                    v.drain()
                    v.tensor_sub(stv[:], stv[:], stq[:])
                    v.drain()
                    v.tensor_scalar(stv[:], stv[:], EPS, None, ALU.add)
                    v.drain()
                    return v.reciprocal(stv[:], stv[:])
                Q["V"].append(fstat)
                vstat = inc("V", sV)
                def fsq(s, v=vstat):
                    s.wait_ge(sV.h, v)
                    return s.activation(sti[:], stv[:], AF.Sqrt)
                Q["S"].append(fsq)
                sdone = inc("S", sS)

                for tt in range(NTT):
                    gt = seq * 100 + sl * NTT + tt
                    def fa(v, tt=tt, sd=sdone, tw=stA.get("xtok%d" % (tt % 2))):
                        v.wait_ge(sS.h, sd)
                        if tw is not None:
                            v.wait_ge(sT.h, tw)
                        return v.tensor_scalar(xn_tok[:, tt % 2, :], xst[:, tt, :],
                                               stm[:, tt:tt + 1], sti[:, tt:tt + 1],
                                               ALU.subtract, ALU.mult)
                    Q["V"].append(fa)
                    va = inc("V", sV)
                    bank = "p%d" % (tt % 2)
                    tp = psT[:, (tt % 2) * 1024:(tt % 2 + 1) * 1024]
                    Q["T"].append(lambda t, v=va: t.wait_ge(sV.h, v))
                    wait_bank("T", bank)
                    def ftr(t, tp=tp, tt=tt):
                        for c in range(NCH):
                            r = t.transpose(tp[:, c * 128:(c + 1) * 128],
                                            xn_tok[:, tt % 2, c * 128:(c + 1) * 128],
                                            identb[:])
                        return r
                    Q["T"].append(ftr)
                    ttr = inc("T", sT)
                    stA["xtok%d" % (tt % 2)] = ttr
                    def fev(v, tp=tp, tt=tt, w=ttr, wf=stA["fm_free_T"]):
                        v.wait_ge(sT.h, w)
                        if wf:
                            v.wait_ge(sT.h, wf)
                        return v.tensor_copy(xn_fm[:, :, tt * 128:(tt + 1) * 128],
                                             tp.rearrange("p (c t) -> p c t", c=NCH))
                    Q["V"].append(fev)
                    vev = inc("V", sV)
                    free_bank(bank, "V", vev)
                stA["xst_free"] = sV.n
                fm_ready = sV.n

                def mm_group(lhs_get, rhs_get, n_k, bank_key, pa, waits, evac, rider=None):
                    for (code, val) in waits:
                        sem = {"V": sV, "T": sT, "G": qG}[code]
                        Q["T"].append(lambda t, s=sem.h, v=val: t.wait_ge(s, v))
                    wait_bank("T", bank_key)
                    def fmm(t, n_k=n_k):
                        for ki in range(n_k):
                            r = t.matmul(pa, lhs_get(ki), rhs_get(ki),
                                         start=(ki == 0),
                                         stop=(ki == n_k - 1 and rider is None))
                        if rider is not None:
                            r = rider(t)
                        return r
                    Q["T"].append(fmm)
                    tmm = inc("T", sT)
                    def fe(v, w=tmm):
                        v.wait_ge(sT.h, w)
                        return evac(v)
                    Q["V"].append(fe)
                    vev = inc("V", sV)
                    free_bank(bank_key, "V", vev)
                    return vev

                def proj(wsb, bias0, dst_fn, waits, rhs=xn_fm):
                    last = None
                    for mo in range(NCH):
                        bk = "m%d" % (mo % 2)
                        pa = ps2[:, (mo % 2) * SLT:(mo % 2 + 1) * SLT]
                        def ev(v, pa=pa, mo=mo, bias0=bias0, dst_fn=dst_fn):
                            return v.tensor_scalar(dst_fn(mo), pa,
                                                   bcols[:, bias0 + mo:bias0 + mo + 1],
                                                   None, ALU.add)
                        last = mm_group(
                            lambda ki, mo=mo, wsb=wsb: wsb[:, ki, mo * 128:(mo + 1) * 128],
                            lambda ki, rhs=rhs: rhs[:, ki, :],
                            NCH, bk, pa, waits, ev)
                    return last

                wfull = qG.n
                uk_done = proj(wk_sb, 8, lambda mo: u_s[:, mo, :],
                               [("V", fm_ready), ("G", wfull)])
                k_done = proj(wl_sb, 24,
                              lambda mo, base=base: k_sb[:, mo, base:base + SLT],
                              [("V", uk_done), ("G", wfull)], rhs=u_s)
                t_after_k = sT.n
                uv_done = proj(wv_sb, 16, lambda mo: u_s[:, mo, :],
                               [("V", fm_ready), ("G", wfull), ("T", t_after_k)])
                for tc in range(4):
                    tkg = sl * 4 + tc
                    for jh in range(2):
                        bk = "m%d" % (jh % 2)
                        pa = ps2[:, (jh % 2) * SLT:(jh % 2 + 1) * SLT]
                        def rider(t, jh=jh, pa=pa, wb=wload_bf):
                            t.wait_ge(qG.h, wb)
                            return t.matmul(pa, ones_bf[0:1, 0:128],
                                            blrow_bf[0:1, jh * SLT:(jh + 1) * SLT],
                                            start=False, stop=True)
                        def ev(v, pa=pa, tkg=tkg, jh=jh):
                            pv = pa.rearrange("p (pr hh c) -> p pr hh c", pr=4, hh=2)
                            v.tensor_copy(vt_sb[:, tkg, jh * 4:(jh + 1) * 4, 0:64],
                                          pv[:, :, 0, :])
                            return v.tensor_copy(
                                vt_sb[:, tkg, jh * 4:(jh + 1) * 4, 65:129],
                                pv[:, :, 1, :])
                        mm_group(
                            lambda ki, tc=tc: u_s[:, ki, tc * 128:(tc + 1) * 128],
                            lambda ki, jh=jh: wl_sb[:, ki, jh * SLT:(jh + 1) * SLT],
                            NCH, bk, pa, [("V", uv_done), ("G", wfull)],
                            ev, rider=rider)
                proj(wq_sb, 0, lambda mo, base=base: q_sb[:, mo, base:base + SLT],
                     [("V", fm_ready), ("G", wfull)])
                stA["fm_free_T"] = sT.n

        stB = {"wo_done": 0, "zdma": 0, "zevT": 0}

        def emit_phase_B(seq):
            psTf = psT.bitcast(F32).reshape([128, 1024])
            stb = [ps1[:, 0:1024], psTf[:, 0:1024]]
            av2 = ps2[:, 0:1024]
            drb = [ps2[:, 1024:1536], ps2[:, 1536:2048]]
            ptfl = pt_sb.reshape([128, 2048])

            def pwide(s):
                if s < 2:
                    return ptfl[:, s * 1024:(s + 1) * 1024]
                return xnt_flat[:, (s - 1) * 1024:s * 1024]

            def pslot(s, h):
                if s < 2:
                    return pt_sb[:, s, h, :]
                o = (s - 1) * 1024 + h * SLT
                return xnt_flat[:, o:o + SLT]
            exp_done = {}
            av_done = {}
            attn_done = {}

            def emit_wo(mo, wst):
                pa = ps2[:, 1024 + (mo % 2) * 512:1024 + (mo % 2 + 1) * 512]
                def fz(t, mo=mo, pa=pa, aa=wst["aa"], wv=wst["zev"].get(mo - 2),
                       wwo=wst["wwo"]):
                    t.wait_ge(sV.h, aa)
                    t.wait_ge(qG.h, wwo)
                    if wv is not None:
                        t.wait_ge(sV.h, wv)
                    for ki in range(NCH):
                        r = t.matmul(pa, wo_sb[:, ki, mo * 128:(mo + 1) * 128],
                                     u_s[:, ki, :], start=(ki == 0), stop=(ki == 7))
                    return r
                Q["T"].append(fz)
                tz = inc("T", sT)
                if mo == NCH - 1:
                    stB["wo_done"] = tz
                def fzev(v, mo=mo, pa=pa, tz=tz, wdma=stB["zdma"]):
                    v.wait_ge(sT.h, tz)
                    if wdma:
                        v.wait_ge(qZ.h, wdma)
                    return v.tensor_scalar(z_sb[:, mo, :], pa,
                                           bcols[:, 32 + mo:32 + mo + 1],
                                           None, ALU.add)
                Q["V"].append(fzev)
                wst["zev"][mo] = inc("V", sV)
                qZ.n += 16
                def fy(y, mo=mo, zv=wst["zev"][mo], seq=wst["seq"],
                       tqc=wst["tqc"]):
                    y.wait_ge(sV.h, zv)
                    y.dma_start(out=dzs[seq, mo, :,
                                       tqc * SLT:(tqc + 1) * SLT],
                                in_=z_sb[:, mo, :]).then_inc(qZ.h, 16)
                Q["Y"].append(fy)
                if mo == NCH - 1:
                    stB["zevT"] = wst["zev"][mo]
                    stB["zdma"] = qZ.n

            pend = []
            for tqc in range(4):
                qb = tqc * SLT
                for j in range(8):
                    par = j % 2
                    avA, avB = av2[:, 0:512], av2[:, 512:1024]

                    def emit_qk(tkc):
                        pk = tkc % 2
                        pp = tkc % 4
                        wk_ = (j, tkc - 2) if tkc >= 2 else \
                            ((j - 1, 14 + tkc) if j > 0 else (7, 14 + tkc))
                        we = exp_done.get(wk_)
                        def fqk(t, j=j, tkc=tkc, qb=qb, pk=pk, we=we):
                            if we is not None:
                                t.wait_ge(sS.h, we)
                            t.matmul(stb[pk][:, 0:512],
                                     k_sb[0:64, j, tkc * 128:(tkc + 1) * 128],
                                     q_sb[0:64, j, qb:qb + SLT], start=True, stop=True,
                                     tile_position=(0, 0))
                            return t.matmul(stb[pk][:, 512:1024],
                                            k_sb[64:128, j, tkc * 128:(tkc + 1) * 128],
                                            q_sb[64:128, j, qb:qb + SLT],
                                            start=True, stop=True,
                                            tile_position=(64, 0))
                        Q["T"].append(fqk)
                        qk_done = inc("T", sT)
                        wav = av_done.get((j, tkc - 4)) if tkc >= 4 else \
                            av_done.get((j - 1, 12 + tkc) if j > 0 else (7, 12 + tkc))
                        def fexp(sc, tkc=tkc, pk=pk, pp=pp, qk=qk_done, wav=wav):
                            sc.wait_ge(sT.h, qk)
                            if wav is not None:
                                sc.wait_ge(sT.h, wav)
                            return sc.activation(pwide(pp), stb[pk][:, 0:1024],
                                                 AF.Exp, bias=cst[:, 1:2],
                                                 scale=cst[:, 0:1])
                        Q["S"].append(fexp)
                        exp_done[(j, tkc)] = inc("S", sS)

                    def emit_av(tkc):
                        pp = tkc % 4
                        ed = exp_done[(j, tkc)]
                        wat = attn_done.get(j - 1 if j >= 1 else 7)
                        def fav(t, j=j, tkc=tkc, pp=pp, ed=ed,
                                avA=avA, avB=avB, wat=wat):
                            t.wait_ge(sS.h, ed)
                            if tkc == 0 and wat is not None:
                                t.wait_ge(sV.h, wat)
                            t.matmul(avA[0:65, :], vt_sb[:, tkc, j, 0:65],
                                     pslot(pp, 0),
                                     start=(tkc == 0), stop=(tkc == 15))
                            return t.matmul(avB[0:65, :], vt_sb[:, tkc, j, 65:130],
                                            pslot(pp, 1),
                                            start=(tkc == 0), stop=(tkc == 15))
                        Q["T"].append(fav)
                        av_done[(j, tkc)] = inc("T", sT)

                    AVLAG = 3
                    for tkc in range(16 + AVLAG):
                        if tkc >= AVLAG:
                            emit_av(tkc - AVLAG)
                        if tkc < 16:
                            emit_qk(tkc)
                        if j == 0 and pend and tkc % 2 == 1:
                            emit_wo(*pend.pop(0))
                    while j == 0 and pend:
                        emit_wo(*pend.pop(0))

                    def fden(v, av2=av2, ad=av_done[(j, 15)]):
                        v.wait_ge(sT.h, ad)
                        return v.reciprocal(rden[0:1, :], av2[64:65, :])
                    Q["V"].append(fden)
                    den_done = inc("V", sV)
                    drp = drb[par]
                    def fdr(t, dd=den_done, drp=drp):
                        t.wait_ge(sV.h, dd)
                        t.matmul(drp[0:64, :], ones_r[0:1, 0:64], rden[0:1, 0:512],
                                 start=True, stop=True, tile_position=(0, 0))
                        return t.matmul(drp[64:128, :], ones_r[0:1, 0:64],
                                        rden[0:1, 512:1024],
                                        start=True, stop=True, tile_position=(0, 64))
                    Q["T"].append(fdr)
                    dr_done = inc("T", sT)
                    def fnorm(v, j=j, drd=dr_done, avA=avA, avB=avB, drp=drp,
                              wz=stB["wo_done"]):
                        v.wait_ge(sT.h, drd)
                        if wz:
                            t = None
                            v.wait_ge(sT.h, wz)
                        v.tensor_copy(dr_sb[:], drp)
                        v.drain()
                        v.scalar_tensor_tensor(u_s[0:64, j, :], avA[0:64, :], 1.0,
                                               dr_sb[0:64, :], ALU.mult, ALU.mult)
                        return v.scalar_tensor_tensor(u_s[64:128, j, :], avB[0:64, :],
                                                      1.0, dr_sb[64:128, :],
                                                      ALU.mult, ALU.mult)
                    Q["V"].append(fnorm)
                    attn_done[j] = inc("V", sV)

                wst = {"aa": attn_done[7], "zev": {}, "wwo": qG.n,
                       "seq": seq, "tqc": tqc}
                if tqc < 3:
                    pend = [(mo, wst) for mo in range(NCH)]
                else:
                    for mo in range(NCH):
                        emit_wo(mo, wst)

        def emit_phase_C(cw):
            ztp = [ps1[:, 0:1024], ps2[:, 1024:2048]]
            ztp_keys = [("zA",), ("zB",)]
            tp_bf = [psT[:, 0:1024], psT[:, 1024:2048]]
            ha_ps = ps2[:, 0:512]
            b7 = ps2[:, 512:1024]
            y_banks = [ps2[:, 0:512], ps2[:, 512:1024]]
            stC = {"zld_free": 0, "xt_free": 0, "x2r_free": 0, "ha_free": 0,
                   "b7_free": 0, "ysb_freeT": 0, "out_dma_all": 0}

            for s in range(NSEQ * NSL):
                seq, sl = divmod(s, NSL)
                base = sl * SLT
                gbase = seq * T + base
                for mo in range(NCH):
                    qZ.n += 16
                    def fy(y, mo=mo, seq=seq, base=base, zf=stC["zld_free"],
                           zb=stB["zdma"], xf=stC["x2r_free"]):
                        if zf:
                            y.wait_ge(sT.h, zf)
                        if zb:
                            y.wait_ge(qZ.h, zb)
                        if xf:
                            y.wait_ge(sV.h, xf)
                        y.dma_start(out=z_ld[:, mo, :],
                                    in_=dzs[seq, mo, :, base:base + SLT]
                                    ).then_inc(qZ.h, 16)
                    Q["Y"].append(fy)
                zall = qZ.n
                ztok_ev = {}
                for tt in range(NTT):
                    pk = tt % 2
                    Q["T"].append(lambda t, w=zall: t.wait_ge(qZ.h, w))
                    wait_bank("T", ztp_keys[pk][0])
                    def ftr(t, tt=tt, pk=pk):
                        for c in range(NCH):
                            r = t.transpose(ztp[pk][:, c * 128:(c + 1) * 128],
                                            z_ld[:, c, tt * 128:(tt + 1) * 128],
                                            identf[:])
                        return r
                    Q["T"].append(ftr)
                    ttr = inc("T", sT)
                    def fev(v, tt=tt, pk=pk, w=ttr):
                        v.wait_ge(sT.h, w)
                        v.drain()
                        for hh in range(2):
                            v.tensor_scalar(z_tok[:, tt, hh * 512:(hh + 1) * 512],
                                            ztp[pk][:, hh * 512:(hh + 1) * 512],
                                            1.0, 0.0, ALU.mult, ALU.add,
                                            accum_out=zsp[:, tt, hh:hh + 1])
                            r = v.drain()
                        return r
                    Q["V"].append(fev)
                    vev = inc("V", sV)
                    free_bank(ztp_keys[pk][0], "V", vev)
                    ztok_ev[tt] = vev
                stC["zld_free"] = sT.n
                def fst2(v):
                    v.drain()
                    v.reduce_sum(st1[:], zsp[:], axis=AX.X)
                    v.drain()
                    for tt in range(NTT):
                        v.scalar_tensor_tensor(dummy[:, 0:D], z_tok[:, tt, :], 1.0,
                                               z_tok[:, tt, :], ALU.mult, ALU.mult,
                                               accum_out=st2[:, tt:tt + 1])
                        v.drain()
                    v.tensor_scalar(stm[:], st1[:], 1.0 / D, None, ALU.mult)
                    v.tensor_scalar(stv[:], st2[:], 1.0 / D, None, ALU.mult)
                    v.drain()
                    v.tensor_tensor(stq[:], stm[:], stm[:], ALU.mult)
                    v.drain()
                    v.tensor_sub(stv[:], stv[:], stq[:])
                    v.drain()
                    v.tensor_scalar(stv[:], stv[:], EPS, None, ALU.add)
                    v.drain()
                    return v.reciprocal(stv[:], stv[:])
                Q["V"].append(fst2)
                v2 = inc("V", sV)
                def fsq2(s, v=v2):
                    s.wait_ge(sV.h, v)
                    return s.activation(sti[:], stv[:], AF.Sqrt)
                Q["S"].append(fsq2)
                sd2 = inc("S", sS)
                def fnm2(v, w=sd2):
                    v.wait_ge(sS.h, w)
                    v.drain()
                    v.scalar_tensor_tensor(stn[:], stm[:], -1.0, sti[:],
                                           ALU.mult, ALU.mult)
                    return v.drain()
                Q["V"].append(fnm2)
                sd2_n = inc("V", sV)
                for tt in range(NTT):
                    qX.n += 16
                    def fy(y, tt=tt, seq=seq, base=base, xf=stC["x2r_free"]):
                        if xf:
                            y.wait_ge(sV.h, xf)
                        y.dma_start(out=xt[:, tt, :],
                                    in_=dx[seq, base + tt * 128:base + (tt + 1) * 128, :]
                                    ).then_inc(qX.h, 16)
                    Q["Y"].append(fy)
                xload_all = qX.n
                x2dma = []
                for tt in range(NTT):
                    def f2a(sc, tt=tt, wn=sd2_n):
                        sc.wait_ge(sV.h, wn)
                        return sc.activation(xn2_t4[:, tt, :], z_tok[:, tt, :],
                                             AF.Identity, bias=stn[:, tt:tt + 1],
                                             scale=sti[:, tt:tt + 1])
                    Q["S"].append(f2a)
                    sx2 = inc("S", sS)
                    def f2(v, tt=tt, sx2=sx2, xl=xload_all):
                        v.wait_ge(sS.h, sx2)
                        v.wait_ge(qX.h, xl)
                        v.drain()
                        v.scalar_tensor_tensor(x2_sl[:, tt, :], xt[:, tt, :], 1.0,
                                               xn2_t4[:, tt, :], ALU.mult, ALU.add,
                                               accum_out=st1[:, tt:tt + 1])
                        v.drain()
                        v.scalar_tensor_tensor(dummy[:, 0:D], x2_sl[:, tt, :],
                                               1.0, x2_sl[:, tt, :],
                                               ALU.mult, ALU.mult,
                                               accum_out=st2[:, tt:tt + 1])
                        return v.drain()
                    Q["V"].append(f2)
                    vx2 = inc("V", sV)
                    if tt == NTT - 1:
                        stC["xt_free"] = vx2
                    qW.n += 16
                    def fy2(y, tt=tt, vx2=vx2, gbase=gbase):
                        y.wait_ge(sV.h, vx2)
                        y.dma_start(out=dx2[gbase + tt * 128:gbase + (tt + 1) * 128, :],
                                    in_=x2_sl[:, tt, :]).then_inc(qW.h, 16)
                    Q["Y"].append(fy2)
                    x2dma.append(qW.n)
                def fst3(v):
                    v.tensor_scalar(stm[:], st1[:], 1.0 / D, None, ALU.mult)
                    v.tensor_scalar(stv[:], st2[:], 1.0 / D, None, ALU.mult)
                    v.drain()
                    v.tensor_tensor(stq[:], stm[:], stm[:], ALU.mult)
                    v.drain()
                    v.tensor_sub(stv[:], stv[:], stq[:])
                    v.drain()
                    v.tensor_scalar(stv[:], stv[:], EPS, None, ALU.add)
                    v.drain()
                    return v.reciprocal(stv[:], stv[:])
                Q["V"].append(fst3)
                v3 = inc("V", sV)
                def fsq3(s, v=v3):
                    s.wait_ge(sV.h, v)
                    return s.activation(sti[:], stv[:], AF.Sqrt)
                Q["S"].append(fsq3)
                sd3 = inc("S", sS)
                def fnm3(v, w=sd3):
                    v.wait_ge(sS.h, w)
                    v.drain()
                    v.scalar_tensor_tensor(stn[:], stm[:], -1.0, sti[:],
                                           ALU.mult, ALU.mult)
                    return v.drain()
                Q["V"].append(fnm3)
                sd3_n = inc("V", sV)
                for tt in range(NTT):
                    def fa3(v, tt=tt, sd3=sd3):
                        v.wait_ge(sS.h, sd3)
                        v.drain()
                        return v.tensor_scalar(xn3_tok[:, tt % 2, :], x2_sl[:, tt, :],
                                               stm[:, tt:tt + 1], sti[:, tt:tt + 1],
                                               ALU.subtract, ALU.mult)
                    Q["V"].append(fa3)
                    va3 = inc("V", sV)
                    bank = "p%d" % (tt % 2)
                    tpb = tp_bf[tt % 2]
                    Q["T"].append(lambda t, v=va3: t.wait_ge(sV.h, v))
                    wait_bank("T", bank)
                    def ftr3(t, tpb=tpb, tt=tt):
                        for c in range(NCH):
                            r = t.transpose(tpb[:, c * 128:(c + 1) * 128],
                                            xn3_tok[:, tt % 2, c * 128:(c + 1) * 128],
                                            identb[:])
                        return r
                    Q["T"].append(ftr3)
                    ttr3 = inc("T", sT)
                    def fev3(v, tpb=tpb, tt=tt, w=ttr3, wf=stC.get("fm_free_T", 0)):
                        v.wait_ge(sT.h, w)
                        if wf:
                            v.wait_ge(sT.h, wf)
                        return v.tensor_copy(xn_fm[:, :, tt * 128:(tt + 1) * 128],
                                             tpb.rearrange("p (c t) -> p c t", c=NCH))
                    Q["V"].append(fev3)
                    vev3 = inc("V", sV)
                    free_bank(bank, "V", vev3)
                fm3_all = sV.n

                def fha(t, fm=fm3_all, wld=cw["astk"], hf=stC["ha_free"]):
                    t.wait_ge(sV.h, fm)
                    t.wait_ge(qG.h, wld)
                    if hf:
                        t.wait_ge(sV.h, hf)
                    for ki in range(NCH):
                        r = t.matmul(ha_ps, astk_a[:, ki, 0:128], xn_fm[:, ki, :],
                                     start=(ki == 0), stop=(ki == 7))
                    return r
                Q["T"].append(fha)
                tha = inc("T", sT)
                def fhl(t, hf=stC["b7_free"]):
                    if hf:
                        t.wait_ge(sV.h, hf)
                    for ki in range(NCH):
                        r = t.matmul(b7[0:6, 0:512], astk_a[:, ki, 128:134],
                                     xn_fm[:, ki, :], start=(ki == 0), stop=(ki == 7))
                    return r
                Q["T"].append(fhl)
                thl = inc("T", sT)
                def flg(v, w=thl):
                    v.wait_ge(sT.h, w)
                    return v.tensor_scalar(lg_sb, b7[0:6, 0:512], rbcol[0:6, :],
                                           None, ALU.add)
                Q["V"].append(flg)
                vlg = inc("V", sV)
                def fltr(t, w=vlg):
                    t.wait_ge(sV.h, w)
                    for tt in range(NTT):
                        r = t.transpose(b7[:, tt * 6:(tt + 1) * 6],
                                        lg_sb[:, tt * 128:(tt + 1) * 128],
                                        identf[0:6, 0:6])
                    return r
                Q["T"].append(fltr)
                tltr = inc("T", sT)
                def flgt(v, w=tltr):
                    v.wait_ge(sT.h, w)
                    return v.tensor_copy(lg_tok[:].rearrange("p t e -> p (t e)"),
                                         b7[:, 0:24])
                Q["V"].append(flgt)
                vlgt = inc("V", sV)
                def fge(sc, w=vlgt):
                    sc.wait_ge(sV.h, w)
                    return sc.activation(ge_sb[:].rearrange("p t e -> p (t e)"),
                                         lg_tok[:].rearrange("p t e -> p (t e)"), AF.Exp)
                Q["S"].append(fge)
                sge = inc("S", sS)
                def fgm(v, w=sge):
                    v.wait_ge(sS.h, w)
                    v.drain()
                    v.reduce_max(g_m1[:], ge_sb[:], axis=AX.X)
                    v.drain()
                    for e in range(6):
                        v.tensor_tensor(g_t1[:, :, e], ge_sb[:, :, e], g_m1[:],
                                        ALU.is_equal)
                        v.drain()
                    v.tensor_mul(g_t2[:], ge_sb[:], g_t1[:])
                    v.drain()
                    v.tensor_sub(g_t2[:], ge_sb[:], g_t2[:])
                    v.drain()
                    v.reduce_max(g_m2[:], g_t2[:], axis=AX.X)
                    v.drain()
                    v.reduce_sum(g_s[:], ge_sb[:], axis=AX.X)
                    v.drain()
                    v.tensor_add(g_rd[:], g_m1[:], g_m2[:])
                    v.drain()
                    v.scalar_tensor_tensor(g_rd[:], g_s[:], 1e-6, g_rd[:],
                                           ALU.mult, ALU.add)
                    v.drain()
                    v.reciprocal(g_rd[:], g_rd[:])
                    v.drain()
                    for e in range(6):
                        v.tensor_tensor(g_t1[:, :, e], ge_sb[:, :, e], g_m2[:],
                                        ALU.is_ge)
                        v.drain()
                    v.tensor_mul(g_t2[:], ge_sb[:], g_t1[:])
                    v.drain()
                    for e in range(6):
                        r = v.tensor_tensor(w_tok[:, :, e], g_t2[:, :, e], g_rd[:],
                                            ALU.mult)
                    return r
                Q["V"].append(fgm)
                vw = inc("V", sV)
                def fwtr(t, w=vw):
                    t.wait_ge(sV.h, w)
                    for tt in range(NTT):
                        r = t.transpose(b7[0:6, tt * 128:(tt + 1) * 128],
                                        w_tok[:, tt, :], identf[:])
                    return r
                Q["T"].append(fwtr)
                twtr = inc("T", sT)
                def fwfm(v, w=twtr):
                    v.wait_ge(sT.h, w)
                    return v.tensor_copy(w_fm, b7[0:6, 0:512])
                Q["V"].append(fwfm)
                vwfm = inc("V", sV)
                def fwrep(t, w=vwfm):
                    t.wait_ge(sV.h, w)
                    return t.matmul(b7, e16[:], w_fm, start=True, stop=True)
                Q["T"].append(fwrep)
                twrep = inc("T", sT)
                def fbr(v, w1=tha, w2=twrep):
                    v.wait_ge(sT.h, w1)
                    v.wait_ge(sT.h, w2)
                    v.tensor_copy(hs_brhs, ha_ps)
                    v.drain()
                    v.tensor_tensor(hs_brhs[32:64, :], hs_brhs[32:64, :],
                                    b7[32:64, :], ALU.mult)
                    return v.tensor_tensor(hs_brhs[64:128, :], hs_brhs[64:128, :],
                                            b7[64:128, :], ALU.mult)
                Q["V"].append(fbr)
                vbr = inc("V", sV)
                stC["ha_free"] = vbr
                stC["b7_free"] = vbr
                yev = {}
                for mo in range(NCH):
                    pa = y_banks[mo % 2]
                    def fy(t, mo=mo, pa=pa, vbr=vbr, fm=fm3_all,
                           wv=yev.get(mo - 2), wl1=cw["bwT"], wl2=cw["bstk"],
                           wts=stC["ysb_freeT"]):
                        t.wait_ge(sV.h, vbr)
                        t.wait_ge(sV.h, fm)
                        t.wait_ge(qG.h, max(wl1, wl2))
                        if wv is not None:
                            t.wait_ge(sV.h, wv)
                        for ki in range(NCH):
                            t.matmul(pa, bw_sb[:, ki, mo * 128:(mo + 1) * 128],
                                     xn_fm[:, ki, :], start=(ki == 0), stop=False)
                        return t.matmul(pa, bstk_a[:, mo, 134:262], hs_brhs,
                                        start=False, stop=True)
                    Q["T"].append(fy)
                    ty = inc("T", sT)
                    def fyev(v, mo=mo, pa=pa, w=ty, wts=stC["ysb_freeT"]):
                        v.wait_ge(sT.h, w)
                        if wts:
                            v.wait_ge(sT.h, wts)
                        return v.tensor_scalar(y_sb[:, mo, :], pa,
                                               bcols[:, 40 + mo:40 + mo + 1],
                                               None, ALU.add)
                    Q["V"].append(fyev)
                    yev[mo] = inc("V", sV)
                y_all = yev[NCH - 1]
                for tt in range(NTT):
                    pk = tt % 2
                    Q["T"].append(lambda t, w=y_all: t.wait_ge(sV.h, w))
                    wait_bank("T", ztp_keys[pk][0])
                    def ftr4(t, tt=tt, pk=pk):
                        for c in range(NCH):
                            r = t.transpose(ztp[pk][:, c * 128:(c + 1) * 128],
                                            y_sb[:, c, tt * 128:(tt + 1) * 128],
                                            identf[:])
                        return r
                    Q["T"].append(ftr4)
                    ttr4 = inc("T", sT)
                    def fev4(v, tt=tt, pk=pk, w=ttr4):
                        v.wait_ge(sT.h, w)
                        v.drain()
                        for hh in range(2):
                            v.tensor_scalar(y_tok[:, tt, hh * 512:(hh + 1) * 512],
                                            ztp[pk][:, hh * 512:(hh + 1) * 512],
                                            1.0, 0.0, ALU.mult, ALU.add,
                                            accum_out=zsp[:, tt, hh:hh + 1])
                            r = v.drain()
                        return r
                    Q["V"].append(fev4)
                    vev4 = inc("V", sV)
                    free_bank(ztp_keys[pk][0], "V", vev4)
                stC["ysb_freeT"] = sT.n
                stC["fm_free_T"] = sT.n
                def fst4(v):
                    v.drain()
                    v.reduce_sum(st1[:], zsp[:], axis=AX.X)
                    v.drain()
                    for tt in range(NTT):
                        v.scalar_tensor_tensor(dummy[:, 0:D], y_tok[:, tt, :], 1.0,
                                               y_tok[:, tt, :], ALU.mult, ALU.mult,
                                               accum_out=st2[:, tt:tt + 1])
                        v.drain()
                    v.tensor_scalar(stm[:], st1[:], 1.0 / D, None, ALU.mult)
                    v.tensor_scalar(stv[:], st2[:], 1.0 / D, None, ALU.mult)
                    v.drain()
                    v.tensor_tensor(stq[:], stm[:], stm[:], ALU.mult)
                    v.drain()
                    v.tensor_sub(stv[:], stv[:], stq[:])
                    v.drain()
                    v.tensor_scalar(stv[:], stv[:], EPS, None, ALU.add)
                    v.drain()
                    return v.reciprocal(stv[:], stv[:])
                Q["V"].append(fst4)
                v4 = inc("V", sV)
                def fsq4(s, v=v4):
                    s.wait_ge(sV.h, v)
                    return s.activation(sti[:], stv[:], AF.Sqrt)
                Q["S"].append(fsq4)
                sd4 = inc("S", sS)
                def fnm4(v, w=sd4):
                    v.wait_ge(sS.h, w)
                    v.drain()
                    v.scalar_tensor_tensor(stn[:], stm[:], -1.0, sti[:],
                                           ALU.mult, ALU.mult)
                    return v.drain()
                Q["V"].append(fnm4)
                sd4_n = inc("V", sV)
                for tt in range(NTT):
                    qX.n += 16
                    def fyx(y, tt=tt, gbase=gbase, ww=x2dma[-1], xf=stC["xt_free"]):
                        y.wait_ge(qW.h, ww)
                        if xf:
                            y.wait_ge(sV.h, xf)
                        y.dma_start(out=x2r[:, tt, :],
                                    in_=dx2[gbase + tt * 128:gbase + (tt + 1) * 128, :]
                                    ).then_inc(qX.h, 16)
                    Q["Y"].append(fyx)
                x2l_all = qX.n
                for tt in range(NTT):
                    def ffin(v, tt=tt, sd4=sd4, xl=x2l_all,
                             wod=stC["out_dma_all"]):
                        v.wait_ge(sS.h, sd4)
                        v.wait_ge(qX.h, xl)
                        if wod:
                            v.wait_ge(qO.h, wod)
                        v.drain()
                        v.tensor_scalar(xn2_t4[:, tt, :], y_tok[:, tt, :],
                                        stm[:, tt:tt + 1], sti[:, tt:tt + 1],
                                        ALU.subtract, ALU.mult)
                        v.drain()
                        return v.tensor_add(out_sb[:, tt, :], x2r[:, tt, :],
                                            xn2_t4[:, tt, :])
                    Q["V"].append(ffin)
                    vfin = inc("V", sV)
                    if tt == NTT - 1:
                        stC["x2r_free"] = vfin
                    qO.n += 16
                    def fyo(y, tt=tt, vfin=vfin, seq=seq, base=base):
                        y.wait_ge(sV.h, vfin)
                        y.dma_start(out=dout[seq, base + tt * 128:base + (tt + 1) * 128, :],
                                    in_=out_sb[:, tt, :]).then_inc(qO.h, 16)
                    Q["Y"].append(fyo)
                stC["out_dma_all"] = qO.n

        phase_list = {"A": "A0", "AB": "A0 B0", "ABA": "A0 B0 A1",
                      "ABAB": "A0 B0 A1 B1",
                      "FULL": "A0 B0 A1 B1 C"}[KPHASES].split()
        for ph in phase_list:
            if ph[0] == "A":
                seq = int(ph[1])
                if seq > 0:
                    load_w("wq", wq_sb)
                emit_phase_A(seq)
                barrier()
                if KDBG and seq == 0:
                    for dst, src in [(ddbg_q, q_sb[:]), (ddbg_k, k_sb[:]),
                                     (ddbg_v, vt_sb[:]), (ddbg_w, wq_sb),
                                     (ddbg_x, xn_fm[:]), (ddbg_t, xn_tok[:]),
                                     (ddbg_s, None)]:
                        if src is None:
                            for i2, tsr in enumerate((st1, st2, stm, stv, sti)):
                                qO.n += 16
                                Q["Y"].append(lambda y, d=ddbg_s[:, i2, :], s2=tsr:
                                              y.dma_start(out=d, in_=s2[:]).then_inc(qO.h, 16))
                            continue
                        qO.n += 16
                        Q["Y"].append(lambda y, d=dst, s=src:
                                      y.dma_start(out=d, in_=s).then_inc(qO.h, 16))
                    barrier()
            elif ph[0] == "B":
                seq = int(ph[1])
                load_w("wo", wo_sb)
                barrier()
                emit_phase_B(seq)
                barrier()
                if KDBG and seq == 0:
                    qO.n += 16
                    Q["Y"].append(lambda y: y.dma_start(out=ddbg_a, in_=u_s[:]
                                                        ).then_inc(qO.h, 16))
                    qO.n += 16
                    Q["Y"].append(lambda y: y.dma_start(out=ddbg_z0, in_=dzs[0]
                                                        ).then_inc(qO.h, 16))
                    barrier()

        cw = {}
        do_c = "C" in phase_list
        for name, dstap, srcap in ([] if not do_c else [
            ("bwT", bw_sb, dw["bwT"].ap().rearrange("(k p) o -> p k o", p=128)),
            ("astk", astk_a[:, :, 0:134],
             dastk.ap().rearrange("(k p) o -> p k o", p=128)),
            ("bstk", bstk_a[:, :, 134:262],
             dbstk.ap().rearrange("p (m c) -> p m c", m=NCH)),
        ]):
            qG.n += 16
            Q["G"].append(lambda g, d=dstap, s=srcap: g.dma_start(out=d, in_=s).then_inc(qG.h, 16))
            cw[name] = qG.n
        if do_c:
            barrier()
            emit_phase_C(cw)

        if KDBG and "C" in phase_list:
            barrier()
            for d_, s_ in [(ddbg_l, lg_tok[:]), (ddbg_g, w_tok[:]), (ddbg_e, ge_sb[:])]:
                qO.n += 16
                Q["Y"].append(lambda y, d=d_, s2=s_: y.dma_start(out=d, in_=s2).then_inc(qO.h, 16))
        totals = [(qC, qC.n), (qX, qX.n), (qZ, qZ.n), (qW, qW.n),
                  (qO, qO.n), (qG, qG.n)]
        def ffinal(g, totals=tuple((t[0].h, t[1]) for t in totals)):
            for h, v in totals:
                if v:
                    g.wait_ge(h, v)
        Q["G"].append(ffinal)

        block = ctx.enter_context(nc.Block())

        @block.tensor
        def _(t):
            for f in Q["T"]:
                f(t)

        @block.vector
        def _(v):
            for f in Q["V"]:
                f(v)

        @block.scalar
        def _(s):
            for f in Q["S"]:
                f(s)

        @block.gpsimd
        def _(g):
            for f in Q["G"]:
                f(g)

        @block.sync
        def _(y):
            for f in Q["Y"]:
                f(y)

    return nc


def _prep(inp):
    f32 = lambda a: np.ascontiguousarray(np.asarray(a), dtype=np.float32)
    x = f32(inp["L0_attn_x"])
    Wq, bq = f32(inp["L0_attn_Wq"]), f32(inp["L0_attn_bq"])
    Wk, bk = f32(inp["L0_attn_Wk"]), f32(inp["L0_attn_bk"])
    Wv, bv = f32(inp["L0_attn_Wv"]), f32(inp["L0_attn_bv"])
    Wl, bl = f32(inp["L0_attn_Wl"]), f32(inp["L0_attn_bl"])
    Wo, bo = f32(inp["L0_attn_Wo"]), f32(inp["L0_attn_bo"])
    bW, bb = f32(inp["L0_ffn_base_W"]), f32(inp["L0_ffn_base_b"])
    shA, shB = f32(inp["L0_ffn_shared_A"]), f32(inp["L0_ffn_shared_B"])
    rA, rB = f32(inp["L0_ffn_routed_A"]), f32(inp["L0_ffn_routed_B"])
    rW, rb = f32(inp["L0_ffn_router_W"]), f32(inp["L0_ffn_router_b"])

    astk = np.zeros((D, 134), np.float32)
    astk[:, 0:16] = shA.T
    for e in range(6):
        astk[:, 32 + 16 * e:48 + 16 * e] = rA[e].T
    astk[:, 128:134] = rW.T
    bstk = np.zeros((128, D), np.float32)
    bstk[0:16] = shB.T
    for e in range(6):
        bstk[32 + 16 * e:48 + 16 * e] = rB[e].T
    bcols = np.zeros((128, 48), np.float32)
    for i, b in enumerate([bq, bk, bv, bl, bo, bb]):
        bcols[:, 8 * i:8 * (i + 1)] = b.reshape(8, 128).T
    rbcol = np.zeros((128, 1), np.float32)
    rbcol[0:6, 0] = rb
    e16aug = np.zeros((6, 128), np.float32)
    for e in range(6):
        e16aug[e, 32 + 16 * e:48 + 16 * e] = 1.0
    consts = np.zeros((128, 4), np.float32)
    consts[:, 0] = 0.125
    consts[:, 1] = -SHIFT
    common = dict(
        wq=Wq, wk=Wk, wv=Wv, wl=Wl, wo=Wo, bwT=np.ascontiguousarray(bW.T),
        astk=astk, bstk=bstk, bcols=bcols, rbcol=rbcol,
        blrow=bl.reshape(1, D), onesr=np.ones((1, 128), np.float32),
        identbf=np.eye(128, dtype=ml_dtypes.bfloat16),
        identf=np.eye(128, dtype=np.float32),
        e16aug=e16aug, consts=consts,
    )
    return x, common


def kernel(**inp):
    x, common = _prep(inp)
    nc = build_nc()
    in_maps = [dict(common, x=np.ascontiguousarray(x[2 * c:2 * c + 2]))
               for c in range(8)]
    res = run_bass_kernel_spmd(nc, in_maps, core_ids=list(range(8)))
    out = np.concatenate([r["out"] for r in res.results], axis=0)
    return out.reshape(16, T, D)
